# version 14
# baseline (speedup 1.0000x reference)
import contextlib
from dataclasses import dataclass

import numpy as np
import concourse.bass as bass
import concourse.mybir as mybir
from concourse.bass_utils import run_bass_kernel_spmd

F32 = mybir.dt.float32
BF16 = mybir.dt.bfloat16
AF = mybir.ActivationFunctionType
ALU = mybir.AluOpType
AX = mybir.AxisListType

NCORES = 8
GRP = 4


def _tiles(total, t=128):
    out = []
    while total > 0:
        out.append(min(t, total))
        total -= out[-1]
    return out


@dataclass
class Cfg:
    B: int = 2
    T: int = 4096
    D: int = 2048
    N: int = 64
    R: int = 128
    PER: int = 512
    E: int = 8
    ISH: int = 5632
    IMOE: int = 1408
    NCH: int = 8
    eps: float = 1e-5

    def __post_init__(self):
        c = self
        assert c.D % (GRP * 128) == 0
        c.DSL = c.D // GRP
        c.NDG = c.DSL // 128
        c.DG = c.D // 128
        c.TC = c.T // c.NCH
        assert c.TC % GRP == 0 and c.TC <= 512
        c.TPC = c.TC // GRP
        assert c.TPC <= 128
        c.GT = 2 * c.TC
        assert c.N % 8 == 0
        c.NBS = 4
        assert c.N % c.NBS == 0
        c.NBLK = c.N // c.NBS
        assert c.ISH % NCORES == 0
        c.ISS = c.ISH // NCORES
        c.PER = min(c.PER, c.T)
        assert c.PER % c.TC == 0
        c.CG = _tiles(c.R + 2 * c.N)
        c.NCG = len(c.CG)
        assert c.R <= 128
        c.sh_tiles = _tiles(c.ISS)
        c.ex_tiles = _tiles(c.IMOE)
        c.dn_tiles = c.sh_tiles + c.ex_tiles
        c.IT = c.ISS + c.IMOE
        c.NTH = max(1, c.GT // 512)
        c.TH = c.GT // c.NTH
        assert c.TH <= 512
        assert c.GT % 128 == 0
        c.NTG = c.TH // 128
        c.NDT = c.D // 512


QUEUES = ("PE", "DVE", "ACT", "POOL", "SYNC")
SERIALIZE = {"DVE", "ACT", "POOL"}


DMA_W = 4
CC_W = 2


class Sched:
    def __init__(self, cc_inc=1):
        self.items = {q: [] for q in QUEUES}
        self.cnt = {}
        self.hwm = {q: {} for q in QUEUES}
        self.cc_inc = cc_inc
        self.didx = {}

    def emit(self, q, kind, fn, deps=(), inc=None, note="", stream=None):
        deps = [d for d in deps if d is not None]
        if kind == "cc":
            idx = self.didx.get("cc", 0)
            self.didx["cc"] = idx + 1
            slot = idx % CC_W
            key = f"cc.{slot}"
            inc = self.cc_inc if inc is None else inc
            if idx >= CC_W:
                deps.append((key, inc * (idx // CC_W)))
            self.cnt[key] = inc * (idx // CC_W + 1)
            tok = (key, self.cnt[key])
        elif kind == "d":
            st = stream or q
            idx = self.didx.get(st, 0)
            self.didx[st] = idx + 1
            slot = idx % DMA_W
            key = f"dma.{st}.{slot}"
            inc = 16
            if idx >= DMA_W:
                deps.append((key, 16 * (idx // DMA_W)))
            self.cnt[key] = 16 * (idx // DMA_W + 1)
            tok = (key, self.cnt[key])
        else:
            key = f"{q}.{kind}"
            if inc is None:
                inc = 1
            if q in SERIALIZE:
                if self.cnt.get(key, 0) > 0:
                    deps.append((key, self.cnt[key]))
            tok = None
            if inc:
                self.cnt[key] = self.cnt.get(key, 0) + inc
                tok = (key, self.cnt[key])
        hw = self.hwm[q]
        wmap = {}
        for (dk, dv) in deps:
            if dv > hw.get(dk, 0):
                wmap[dk] = max(wmap.get(dk, 0), dv)
        for dk, dv in wmap.items():
            hw[dk] = dv
        self.items[q].append((sorted(wmap.items()), fn,
                              key if (inc or 0) else None, inc, note))
        return tok

    def wait_only(self, q, deps, note=""):
        self.emit(q, "c", None, deps, inc=0, note=note)


class Buf:
    __slots__ = ("t", "w", "r")

    def __init__(self, t):
        self.t = t
        self.w = []
        self.r = []

    def rd(self):
        return list(self.w)

    def wd(self):
        return list(self.w) + list(self.r)

    def read_by(self, tok):
        if tok is not None:
            self.r.append(tok)
            if len(self.r) > 64:
                del self.r[:32]

    def written_by(self, tok):
        if tok is not None:
            self.w.append(tok)
            if len(self.w) > 64:
                del self.w[:32]
            self.r = []


class Builder:
    def __init__(self, cfg: Cfg, cc_inc=1):
        self.c = cfg
        self.nc = bass.Bass("TRN2", target_bir_lowering=False)
        self.s = Sched(cc_inc)
        self.es = contextlib.ExitStack()
        self._pend = {}

    def sb(self, name, shape, dtype):
        return Buf(self.es.enter_context(self.nc.sbuf_tensor(name, shape, dtype)))

    def psum(self, name, shape, dtype=F32):
        return Buf(self.es.enter_context(self.nc.psum_tensor(name, shape, dtype)))

    def dram(self, name, shape, dtype, kind="Internal"):
        return Buf(self.nc.dram_tensor(name, shape, dtype, kind=kind))

    def op(self, q, fn, ins=(), outs=(), extra=(), note=""):
        deps = list(extra)
        for b in ins:
            deps += b.rd()
        for b in outs:
            deps += b.wd()
        tok = self.s.emit(q, "c", fn, deps, note=note)
        for b in ins:
            b.read_by(tok)
        for b in outs:
            b.written_by(tok)
        return tok

    def mm(self, fn, ins=(), psum_out=None, first=False, last=False,
           extra=(), note="", force_tok=False):
        deps = list(extra)
        for b in ins:
            deps += b.rd()
        if first and psum_out is not None:
            deps += psum_out.wd()
        tok = self.s.emit("PE", "c", fn, deps,
                          inc=(1 if (last or force_tok) else 0), note=note)
        if force_tok and not last:
            for b in ins:
                b.read_by(tok)
        if psum_out is not None:
            pend = self._pend.setdefault(id(psum_out), [])
            pend.extend(ins)
            if last:
                for b in {id(x): x for x in pend}.values():
                    b.read_by(tok)
                pend.clear()
                psum_out.written_by(tok)
        else:
            for b in ins:
                b.read_by(tok)
        return tok

    def dma(self, q, dst, src, dst_ap=None, src_ap=None, extra=(), note="",
            stream=None, **kw):
        deps = list(extra) + src.rd() + dst.wd()
        d_ap = dst_ap if dst_ap is not None else dst.t.ap()
        s_ap = src_ap if src_ap is not None else src.t.ap()

        def fn(e, _d=d_ap, _s=s_ap, _kw=kw):
            return e.dma_start(out=_d, in_=_s, **_kw)

        tok = self.s.emit(q, "d", fn, deps, note=note, stream=stream)
        src.read_by(tok)
        dst.written_by(tok)
        return tok

    def coll(self, kind, alu, groups, src, dst, extra=(), note=""):
        deps = list(extra) + src.rd() + dst.wd()

        def fn(e, _k=kind, _a=alu, _g=groups, _s=src.t, _d=dst.t):
            return e.collective_compute(
                _k, _a, replica_groups=_g,
                ins=[_s.ap().opt()], outs=[_d.ap().opt()])

        tok = self.s.emit("POOL", "cc", fn, deps, note=note)
        src.read_by(tok)
        dst.written_by(tok)
        return tok

    DMA_STREAMS = ("ls", "sw", "lx", "lo", "lw", "sa", "bc", "so")

    def build(self):
        c = self.c
        nc = self.nc
        dram, sb, psum = self.dram, self.sb, self.psum

        self.x_fm = dram("x_fm", [c.D, c.T], F32, "ExternalInput")
        self.x_own = dram("x_own", [c.NCH * c.TPC, c.D], F32, "ExternalInput")
        self.ln1w = dram("ln1w", [128, c.DG], F32, "ExternalInput")
        self.ln1b = dram("ln1b", [128, c.DG], F32, "ExternalInput")
        self.xpwT = dram("xpwT", [c.D, c.R + 2 * c.N], F32, "ExternalInput")
        self.dtwT = dram("dtwT", [c.R, c.DSL], F32, "ExternalInput")
        self.dtb = dram("dtb", [128, c.NDG], F32, "ExternalInput")
        self.alog = dram("alog", [128, c.NDG * c.N], F32, "ExternalInput")
        self.dpar = dram("dpar", [128, c.NDG], F32, "ExternalInput")
        self.woutT = dram("woutT", [c.DSL, c.D], F32, "ExternalInput")
        self.wr9T = dram("wr9T", [c.D, c.E + 1], F32, "ExternalInput")
        self.esel = dram("esel", [128, c.E], F32, "ExternalInput")
        self.eye = dram("eye", [128, 128], F32, "ExternalInput")
        self.ws_gu = dram("ws_gu", [c.D, 2 * c.ISS], F32, "ExternalInput")
        self.ws_dn = dram("ws_dn", [c.ISS, c.D], F32, "ExternalInput")
        self.we_gu = dram("we_gu", [c.D, 2 * c.IMOE], F32, "ExternalInput")
        self.we_dn = dram("we_dn", [c.IMOE, c.D], F32, "ExternalInput")
        self.out = dram("out", [c.NCH * c.TPC, c.D], F32, "ExternalOutput")

        self.wc_gu = dram("wc_gu", [128, c.DG * 2 * c.IT], BF16)
        self.wc_dn = dram("wc_dn", [c.IT, c.D], BF16)
        self.wc_wo = dram("wc_wo", [128, c.NDG * c.D], BF16)
        self.brows = [dram(f"brows{i}", [c.N, c.TC], BF16) for i in range(2)]
        self.crows = dram("crows", [c.N, c.PER], BF16)
        self.rowbc = [dram(f"rowbc{i}", [1, 2 * c.TC], BF16) for i in range(2)]
        self.rs1i = [dram(f"rs1i{i}", [c.TC, c.D], F32) for i in range(2)]
        self.rs1o = [dram(f"rs1o{i}", [c.TPC, c.D], F32) for i in range(2)]
        self.agi = [dram(f"agi{i}", [128, c.DG * c.TPC], BF16) for i in range(2)]
        self.ago = [Buf(self.nc.dram_tensor(f"ago{i}", [NCORES * 128, c.DG * c.TPC], BF16, kind="Internal", addr_space="Shared")) for i in range(2)]
        self.cwrowd = [dram(f"cwrowd{i}", [2, c.GT], BF16) for i in range(2)]
        self.rs2i = [dram(f"rs2i{i}", [c.GT, c.D], F32) for i in range(2)]
        self.rs2o = [dram(f"rs2o{i}", [c.TPC, c.D], F32) for i in range(2)]
        self.x1d = dram("x1d", [c.NCH * c.TPC, c.D], F32)

        self.s_eyef = sb("s_eyef", [128, 128], F32)
        self.s_ones = sb("s_ones", [128, 1], BF16)
        self.s_eps = sb("s_eps", [128, 1], F32)
        self.s_ln1w = sb("s_ln1w", [128, c.DG], F32)
        self.s_ln1wn = sb("s_ln1wn", [128, c.DG], F32)
        self.s_ln1b = sb("s_ln1b", [128, c.DG], F32)
        self.s_dtb = sb("s_dtb", [128, c.NDG], F32)
        self.s_negA = sb("s_negA", [128, c.NDG * c.N], F32)
        self.s_dpar = sb("s_dpar", [128, c.NDG], F32)
        self.s_esel = sb("s_esel", [128, c.E], F32)
        self.s_xpwT = sb("s_xpwT", [128, c.DG * (c.R + 2 * c.N)], BF16)
        self.s_dtwT = sb("s_dtwT", [c.R, c.DSL], BF16)
        self.s_wr9T = sb("s_wr9T", [128, c.DG * (c.E + 1)], BF16)

        self.s_xg = [sb(f"s_xg{i}", [128, c.TC], F32) for i in range(2)]
        self.s_xbf = sb("s_xbf", [128, c.DG * c.TC], BF16)
        self.s_sqb = [sb(f"s_sqb{i}", [128, c.TC], BF16) for i in range(2)]
        self.s_xn = [sb(f"s_xn{i}", [128, c.TC], BF16) for i in range(2)]
        self.s_xnsl = sb("s_xnsl", [128, c.NDG * c.TC], BF16)
        self.s_row = sb("s_row", [1, 2 * c.TC], F32)
        self.s_rowb = sb("s_rowb", [1, 2 * c.TC], BF16)
        self.s_rbc = sb("s_rbc", [128, 2 * c.TC], BF16)
        self.s_prT = sb("s_prT", [128, c.NCG * c.TC], BF16)
        self.s_delta = sb("s_delta", [128, c.NDG * c.TC], BF16)
        self.s_wsc = sb("s_wsc", [128, c.NDG * c.TC], BF16)
        self.s_dA = sb("s_dA", [128, c.NBS * (c.TC + 2)], F32)
        self.s_u = [sb(f"s_u{i}", [128, c.NBS * (c.TC + 2)], BF16) for i in range(2)]
        self.s_h = sb("s_h", [128, c.NBS * (c.TC + 2)], BF16)
        self.s_g = self.s_u
        self.s_g2 = self.s_h
        self.s_y = sb("s_y", [128, c.NDG * c.TC], F32)
        self.s_hc = sb("s_hc", [128, c.NDG * c.N], F32)
        self.s_bbc = [sb(f"s_bbc{i}", [128, c.NBS * c.TC], BF16) for i in range(2)]
        self.s_cbc = [sb(f"s_cbc{i}", [128, c.NBS * (c.TC + 2)], BF16) for i in range(2)]
        self.s_z = sb("s_z", [128, c.NDG * c.TC], BF16)
        self.s_opv = [sb(f"s_opv{i}", [128, 512], F32) for i in range(2)]
        self.s_wo2 = [sb(f"s_wo2{i}", [128, c.NDG * 512], BF16) for i in range(2)]

        self.s_mo = [sb("s_mo0", [c.TPC, c.D], F32)] * 2
        self.s_x1f = sb("s_x1f", [c.TPC, c.D], F32)
        self.s_x1b = sb("s_x1b", [c.TPC, c.D], BF16)
        self.s_st1 = sb("s_st1", [c.TPC, 8], F32)
        self.s_x1fm = sb("s_x1fm", [128, c.DG * c.TPC], BF16)
        self.s_agst = self.s_x1fm

        self.s_x1g = sb("s_x1g", [128, c.DG * c.TH], BF16)
        self.s_wg = sb("s_wg", [128, c.DG * 128], BF16)
        self.s_wu = sb("s_wu", [128, c.DG * 128], BF16)
        self.s_mid = sb("s_mid", [128, len(c.dn_tiles) * c.TH], BF16)
        self.s_sg = sb("s_sg", [128, c.TH], BF16)
        self.s_ue = sb("s_ue", [128, c.TH], BF16)
        self.s_wdn2 = [sb(f"s_wdn{i}", [128, 512], BF16) for i in range(2)]
        self.s_o2 = [sb(f"s_o2{i}", [128, 512], F32) for i in range(2)]
        self.s_lg = sb("s_lg", [128, c.E + 1], F32)
        self.s_pr = sb("s_pr", [128, c.E], F32)
        self.s_t2 = sb("s_t2", [128, 2 * c.E], F32)
        self.s_tc = sb("s_tc", [128, 8], F32)
        s_cwsg_decl = self.s_cwsg = sb("s_cwsg", [128, 2], F32)
        self.s_mrv = sb("s_mrv", [128, 8], F32)
        self.s_cwst = sb("s_cwst", [2, c.GT], BF16)
        self.s_cwbc = sb("s_cwbc", [128, c.GT], BF16)
        self.s_sgbc = sb("s_sgbc", [128, c.GT], BF16)

        self.s_fin = sb("s_fin", [c.TPC, c.D], F32)
        if c.TPC == 128:
            self.s_stg = self.s_mo[0]
            self.s_stg2 = self.s_fin
        else:
            self.s_stg = sb("s_stg", [128, max(512, c.D)], F32)
            self.s_stg2 = sb("s_stg2", [128, max(512, c.D)], F32)
        self.s_stgb = self.s_x1fm
        self.s_stgb2 = self.s_mid
        self.s_fst = sb("s_fst", [c.TPC, 8], F32)
        self.s_fdmp = self.s_x1b

        self.p_mm = [psum(f"p_mm{i}", [128, 512]) for i in range(2)]
        self.p_op = [psum(f"p_op{i}", [128, 512]) for i in range(2)]
        self.p_g = psum("p_g", [128, 512])
        self.p_u = psum("p_u", [128, 512])
        self.p_o2 = [psum(f"p_o2{i}", [128, 512]) for i in range(2)]

        self.fin_toks = []
        self._setup()
        DONE = object()
        wg = self._wcache_gen()
        while getattr(self, "_wc_phase", 0) < 1:
            if next(wg, DONE) is DONE:
                wg = None
                break
        for j in range(c.NCH + 5):
            jz, jx, jf, jo = j - 1, j - 2, j - 3, j - 4
            if 0 <= jz < c.NCH:
                self._z_outproj(jz)
                self._rs1(jz)
            if j < c.NCH:
                self._mamba_pre(j)
            sg = self._scan_gen(j) if j < c.NCH else None
            if 0 <= jf < c.NCH and wg is not None:
                while next(wg, DONE) is not DONE:
                    pass
                wg = None
            fg = self._ffn_gen(jf) if 0 <= jf < c.NCH else None
            if sg is not None:
                next(sg, DONE)
            if 0 <= jx < c.NCH:
                self._x1_stage(jx)
                self._ag(jx)
            if 0 <= jo < c.NCH:
                self._final(jo)
            while sg is not None or fg is not None:
                if sg is not None and next(sg, DONE) is DONE:
                    sg = None
                if wg is not None and next(wg, DONE) is DONE:
                    wg = None
                if fg is not None and next(fg, DONE) is DONE:
                    fg = None
            if 0 <= jf < c.NCH:
                self._rs2(jf)
            if j >= c.NCH and wg is not None:
                while next(wg, DONE) is not DONE:
                    pass
                wg = None
        self.s.wait_only("SYNC", self.fin_toks, note="output completion")
        self._replay()
        return nc

    def _cast_via(self, q, dstbuf, dst_ap, srcbuf, src_ap, note=""):
        def fn(e, _d=dst_ap, _s=src_ap):
            if q == "ACT":
                return e.activation(out=_d, in_=_s, func=AF.Copy)
            return e.tensor_copy(_d, _s)
        return self.op(q, fn, ins=(srcbuf,), outs=(dstbuf,), note=note)

    def _setup(self):
        c, s = self.cfg if False else self.c, self
        b = None
        QW = "SYNC"
        loads = [(self.s_ln1w, self.ln1w), (self.s_ln1b, self.ln1b),
                 (self.s_dtb, self.dtb), (self.s_dpar, self.dpar),
                 (self.s_esel, self.esel), (self.s_eyef, self.eye)]
        for dstb, srcb in loads:
            self.dma("SYNC", dstb, srcb, stream="ls")
        self.op("DVE", lambda e: e.memset(self.s_ones.t[:, :], 1.0),
                outs=(self.s_ones,))
        self.op("DVE", lambda e: e.memset(self.s_hc.t[:, :], 0.0),
                outs=(self.s_hc,))
        self.op("DVE", lambda e: e.memset(self.s_eps.t[:, :], float(self.c.eps)),
                outs=(self.s_eps,))
        self.op("DVE", lambda e: e.memset(self.s_mrv.t[:, :], -1.0),
                outs=(self.s_mrv,))
        self.op("DVE", lambda e: e.memset(self.s_dA.t[:, :], 0.0),
                outs=(self.s_dA,))
        for cb in self.s_cbc:
            self.op("DVE", lambda e, _cb=cb: e.memset(_cb.t[:, :], 0.0),
                    outs=(cb,))
        self.op("DVE", lambda e: e.tensor_scalar_mul(
            self.s_ln1wn.t[:, :], self.s_ln1w.t[:, :], -1.0),
            ins=(self.s_ln1w,), outs=(self.s_ln1wn,))
        self.dma("SYNC", self.s_stg2, self.alog, stream="ls",
                 dst_ap=self.s_stg2.t[:, 0:c.NDG * c.N])
        self.op("ACT", lambda e: e.activation(
            out=self.s_negA.t[:, :], in_=self.s_stg2.t[:, 0:c.NDG * c.N],
            func=AF.Exp), ins=(self.s_stg2,), outs=(self.s_negA,))
        self.op("DVE", lambda e: e.tensor_scalar_mul(
            self.s_negA.t[:, :], self.s_negA.t[:, :], -1.0),
            ins=(self.s_negA,), outs=(self.s_negA,))


    def _wcache_gen(self):
        c = self.c
        self._wc_phase = 0
        pairs = [(self.s_stg, self.s_stgb), (self.s_stg2, self.s_stgb2)]
        pair_cols = [min(p[0].t.ap().shape[1], p[1].t.ap().shape[1])
                     for p in pairs]
        self._pi = 0

        self._wc_count = 0

        def cast_piece(srcb, src_ap_fn, width, stores):
            pi = self._pi
            self._pi += 1
            self._wc_count += 1
            stg, stgb = pairs[pi % 2]
            self.dma("SYNC", stg, srcb, stream="ls",
                     dst_ap=stg.t[0:src_ap_fn.__rows__, 0:width],
                     src_ap=src_ap_fn(width))
            self._cast_via("ACT", stgb,
                           stgb.t[0:src_ap_fn.__rows__, 0:width],
                           stg, stg.t[0:src_ap_fn.__rows__, 0:width])
            for dstb, dst_ap, col0, w in stores:
                self.dma("ACT", dstb, stgb, stream="sw", dst_ap=dst_ap,
                         src_ap=stgb.t[0:src_ap_fn.__rows__,
                                       col0:col0 + w])

        def srcfn(srcb, r0, r1, c0):
            def f(width):
                return srcb.t[r0:r1, c0:c0 + width]
            f.__rows__ = r1 - r0
            return f

        CGA = c.R + 2 * c.N
        for g in range(c.DG):
            cast_piece(self.xpwT, srcfn(self.xpwT, g * 128, (g + 1) * 128, 0),
                       CGA, [(self.s_xpwT,
                              self.s_xpwT.t[:, g * CGA:(g + 1) * CGA], 0,
                              CGA)])
            if self._wc_count % 4 == 0:
                yield
        cast_piece(self.dtwT, srcfn(self.dtwT, 0, c.R, 0), c.DSL,
                   [(self.s_dtwT, self.s_dtwT.t[:, :], 0, c.DSL)])
        E1 = c.E + 1
        for g in range(c.DG):
            cast_piece(self.wr9T, srcfn(self.wr9T, g * 128, (g + 1) * 128, 0),
                       E1, [(self.s_wr9T,
                             self.s_wr9T.t[:, g * E1:(g + 1) * E1], 0, E1)])
        for g in range(c.NDG):
            col = 0
            while col < c.D:
                w = min(min(pair_cols), c.D - col)
                cast_piece(self.woutT,
                           srcfn(self.woutT, g * 128, (g + 1) * 128, col), w,
                           [(self.wc_wo,
                             self.wc_wo.t[:, g * c.D + col:g * c.D + col + w],
                             0, w)])
                col += w
                if self._wc_count % 4 == 0:
                    yield
        self._wc_phase = 1
        yield
        secs = []
        blk0 = 0
        sec_defs = [(self.ws_gu, 0, c.sh_tiles), (self.ws_gu, c.ISS, c.sh_tiles),
                    (self.we_gu, 0, c.ex_tiles), (self.we_gu, c.IMOE, c.ex_tiles)]
        tile_list = []
        for srcb, soff, tiles in sec_defs:
            off = 0
            for w in tiles:
                tile_list.append((srcb, soff + off, w, blk0))
                off += w
                blk0 += c.DG * w
        self.gu_tile_starts = tile_list
        for g in range(c.DG):
            for srcb, soff, tiles in sec_defs:
                width = sum(tiles)
                col = 0
                while col < width:
                    w = min(min(pair_cols), width - col)
                    stores = []
                    toff = 0
                    for (tsrc, tscol, tw, tblk) in tile_list:
                        if tsrc is srcb and tscol >= soff and \
                                tscol < soff + width:
                            rel = tscol - soff
                            if rel >= col and rel + tw <= col + w:
                                stores.append((self.wc_gu,
                                               self.wc_gu.t[:, tblk + g * tw:
                                                            tblk + (g + 1) * tw],
                                               rel - col, tw))
                    exp_tiles = sum(1 for (tsrc, tscol, tw, _tb) in tile_list
                                    if tsrc is srcb and soff <= tscol <
                                    soff + width and col <= tscol - soff and
                                    tscol - soff + tw <= col + w)
                    assert len(stores) == exp_tiles and stores, \
                        (g, soff, col, w)
                    cast_piece(srcb,
                               srcfn(srcb, g * 128, (g + 1) * 128, soff + col),
                               w, stores)
                    col += w
                    if self._wc_count % 4 == 0:
                        yield
        row = 0
        for srcb, nrows in ((self.ws_dn, c.ISS), (self.we_dn, c.IMOE)):
            done = 0
            while done < nrows:
                r = min(128, nrows - done)
                col = 0
                while col < c.D:
                    w = min(min(pair_cols), c.D - col)
                    cast_piece(srcb, srcfn(srcb, done, done + r, col), w,
                               [(self.wc_dn,
                                 self.wc_dn.t[row:row + r, col:col + w],
                                 0, w)])
                    col += w
                    if self._wc_count % 4 == 0:
                        yield
                done += r
                row += r


    def _mamba_pre(self, j):
        c = self.c
        t0 = j * c.TC
        TC = c.TC
        for g in range(c.DG):
            xg = self.s_xg[g % 2]
            sq = self.s_sqb[g % 2]
            xb = self.s_xbf.t[:, g * TC:(g + 1) * TC]
            self.dma("SYNC", xg, self.x_fm, stream="lx",
                     src_ap=self.x_fm.t[g * 128:(g + 1) * 128, t0:t0 + c.TC])
            self.op("ACT", lambda e, _x=xg, _xb=xb: e.activation(
                out=_xb, in_=_x.t[:, :], func=AF.Copy),
                ins=(xg,), outs=(self.s_xbf,))
            self.op("ACT", lambda e, _s=sq, _x=xg: e.activation(
                out=_s.t[:, :], in_=_x.t[:, :], func=AF.Square),
                ins=(xg,), outs=(sq,))
            self.mm(lambda e, _xb=xb, _g=g: e.matmul(
                self.p_mm[0].t[0:1, 0:c.TC], self.s_ones.t[:, :], _xb,
                start=(_g == 0), stop=(_g == c.DG - 1),
                skip_group_check=True),
                ins=(self.s_xbf, self.s_ones), psum_out=self.p_mm[0],
                first=(g == 0), last=(g == c.DG - 1), note=f"sum{j}",
                force_tok=True)
            self.mm(lambda e, _s=sq, _g=g: e.matmul(
                self.p_mm[1].t[0:1, 0:c.TC], self.s_ones.t[:, :], _s.t[:, :],
                start=(_g == 0), stop=(_g == c.DG - 1),
                skip_group_check=True),
                ins=(sq, self.s_ones), psum_out=self.p_mm[1],
                first=(g == 0), last=(g == c.DG - 1), note=f"sq{j}",
                force_tok=True)
        rowt = self.s_row.t
        self.op("DVE", lambda e: e.tensor_scalar_mul(
            rowt[0:1, 0:TC], self.p_mm[0].t[0:1, 0:TC], 1.0 / c.D),
            ins=(self.p_mm[0],), outs=(self.s_row,))
        self.op("DVE", lambda e: e.tensor_scalar_mul(
            rowt[0:1, TC:2 * TC], self.p_mm[1].t[0:1, 0:TC], 1.0 / c.D),
            ins=(self.p_mm[1],), outs=(self.s_row,))
        self.op("DVE", lambda e: e.tensor_tensor(
            self.p_mm[0].t[0:1, 0:TC], rowt[0:1, 0:TC], rowt[0:1, 0:TC],
            ALU.mult), ins=(self.s_row,), outs=(self.p_mm[0],))
        self.op("DVE", lambda e: e.tensor_tensor(
            rowt[0:1, TC:2 * TC], rowt[0:1, TC:2 * TC],
            self.p_mm[0].t[0:1, 0:TC], ALU.subtract),
            ins=(self.s_row, self.p_mm[0]), outs=(self.s_row,))
        self.op("ACT", lambda e: e.activation(
            out=rowt[0:1, TC:2 * TC], in_=rowt[0:1, TC:2 * TC], func=AF.Ln,
            bias=self.s_eps.t[0:1, 0:1]),
            ins=(self.s_row, self.s_eps), outs=(self.s_row,))
        self.op("ACT", lambda e: e.activation(
            out=rowt[0:1, TC:2 * TC], in_=rowt[0:1, TC:2 * TC], func=AF.Exp,
            scale=-0.5), ins=(self.s_row,), outs=(self.s_row,))
        self.op("DVE", lambda e: e.tensor_tensor(
            rowt[0:1, 0:TC], rowt[0:1, 0:TC], rowt[0:1, TC:2 * TC], ALU.mult),
            ins=(self.s_row,), outs=(self.s_row,))
        self.op("DVE", lambda e: e.tensor_copy(
            self.s_rowb.t[:, :], rowt[0:1, 0:2 * TC]),
            ins=(self.s_row,), outs=(self.s_rowb,))
        rb = self.rowbc[j % 2]
        self.dma("ACT", rb, self.s_rowb, stream="sa",
                 src_ap=self.s_rowb.t[0:1, :])
        self.dma("ACT", self.s_rbc, rb, stream="bc",
                 src_ap=rb.t.ap()[0:1, 0:TC].to_broadcast([128, TC]),
                 dst_ap=self.s_rbc.t[:, 0:TC])
        self.dma("ACT", self.s_rbc, rb, stream="bc",
                 src_ap=rb.t.ap()[0:1, TC:2 * TC].to_broadcast([128, TC]),
                 dst_ap=self.s_rbc.t[:, TC:2 * TC])

        CGA = c.R + 2 * c.N
        for g in range(c.DG):
            xn = self.s_xn[g % 2]
            t1 = self.s_sqb[g % 2]
            xb = self.s_xbf.t[:, g * TC:(g + 1) * TC]
            self.op("DVE", lambda e, _xb=xb, _g=g, _t=t1:
                    e.scalar_tensor_tensor(
                out=_t.t[:, :], in0=_xb,
                scalar=self.s_ln1w.t[:, _g:_g + 1],
                in1=self.s_rbc.t[:, TC:2 * TC], op0=ALU.mult, op1=ALU.mult),
                ins=(self.s_xbf, self.s_ln1w, self.s_rbc), outs=(t1,))
            self.op("DVE", lambda e, _g=g, _t=t1, _xn=xn:
                    e.scalar_tensor_tensor(
                out=_xn.t[:, :], in0=self.s_rbc.t[:, 0:TC],
                scalar=self.s_ln1wn.t[:, _g:_g + 1],
                in1=_t.t[:, :], op0=ALU.mult, op1=ALU.add),
                ins=(self.s_rbc, self.s_ln1wn, t1), outs=(xn,))
            self.op("DVE", lambda e, _xn=xn, _g=g: e.tensor_scalar(
                out=_xn.t[:, :], in0=_xn.t[:, :],
                scalar1=self.s_ln1b.t[:, _g:_g + 1], scalar2=None,
                op0=ALU.add), ins=(xn, self.s_ln1b), outs=(xn,))
            if g < c.NDG:
                self.op("DVE", lambda e, _xn=xn, _d=g: e.tensor_copy(
                    self.s_xnsl.t[:, _d * TC:(_d + 1) * TC], _xn.t[:, :]),
                    ins=(xn,), outs=(self.s_xnsl,))
            co = 0
            for ci, cw in enumerate(c.CG):
                self.mm(lambda e, _xn=xn, _g=g, _ci=ci, _co=co, _cw=cw:
                        e.matmul(
                            self.p_mm[_ci % 2].t[0:_cw, 0:c.TC],
                            self.s_xpwT.t[:, _g * CGA + _co:_g * CGA + _co + _cw],
                            _xn.t[:, :],
                            start=(_g == 0), stop=(_g == c.DG - 1),
                            skip_group_check=True),
                        ins=(xn, self.s_xpwT), psum_out=self.p_mm[ci % 2],
                        first=(g == 0), last=(g == c.DG - 1),
                        note=f"proj{j}.{ci}", force_tok=True)
                co += cw
        for ci, cw in enumerate(c.CG):
            self.op("ACT", lambda e, _ci=ci, _cw=cw: e.activation(
                out=self.s_prT.t[0:_cw, _ci * TC:(_ci + 1) * TC],
                in_=self.p_mm[_ci % 2].t[0:_cw, 0:TC], func=AF.Copy),
                ins=(self.p_mm[ci % 2],), outs=(self.s_prT,))
        bseg = self._seg_ap(c.R, c.N)
        cseg = self._seg_ap(c.R + c.N, c.N)
        self.dma("ACT", self.brows[j % 2], self.s_prT, stream="sa",
                 src_ap=bseg)
        if (j * c.TC) < c.PER:
            self.dma("ACT", self.crows, self.s_prT, stream="sa", src_ap=cseg,
                     dst_ap=self.crows.t[:, j * c.TC:(j + 1) * c.TC])
        for dg in range(c.NDG):
            pm = self.p_mm[dg % 2]
            self.mm(lambda e, _dg=dg, _pm=pm: e.matmul(
                _pm.t[:, 0:c.TC], self.s_dtwT.t[:, _dg * 128:(_dg + 1) * 128],
                self.s_prT.t[0:c.R, 0:TC], start=True, stop=True,
                skip_group_check=True),
                ins=(self.s_dtwT, self.s_prT), psum_out=pm, first=True,
                last=True, note=f"dt{j}.{dg}")
            dcol = self.s_delta.t[:, dg * TC:(dg + 1) * TC]
            sa = self.s_sqb[0]
            sl = self.s_sqb[1]
            self.op("DVE", lambda e, _pm=pm, _dg=dg, _d=dcol: e.tensor_scalar(
                out=_d, in0=_pm.t[:, 0:TC],
                scalar1=self.s_dtb.t[:, _dg:_dg + 1], scalar2=None,
                op0=ALU.add),
                ins=(pm, self.s_dtb), outs=(self.s_delta,))
            self.op("ACT", lambda e, _d=dcol, _sa=sa: e.activation(
                out=_sa.t[:, :], in_=_d, func=AF.Abs),
                ins=(self.s_delta,), outs=(sa,))
            self.op("ACT", lambda e, _sa=sa, _sl=sl: e.activation(
                out=_sl.t[:, :], in_=_sa.t[:, :],
                func=AF.Exp, scale=-1.0), ins=(sa,), outs=(sl,))
            self.op("ACT", lambda e, _sl=sl: e.activation(
                out=_sl.t[:, :], in_=_sl.t[:, :],
                func=AF.Ln, bias=1.0), ins=(sl,), outs=(sl,))
            self.op("DVE", lambda e, _d=dcol: e.tensor_scalar_max(_d, _d, 0.0),
                    ins=(self.s_delta,), outs=(self.s_delta,))
            self.op("DVE", lambda e, _d=dcol, _sl=sl: e.tensor_tensor(
                _d, _d, _sl.t[:, :], ALU.add),
                ins=(self.s_delta, sl), outs=(self.s_delta,))
            self.op("POOL", lambda e, _d=dcol, _dg=dg: e.tensor_tensor(
                self.s_wsc.t[:, _dg * TC:(_dg + 1) * TC], _d,
                self.s_xnsl.t[:, _dg * TC:(_dg + 1) * TC], ALU.mult),
                ins=(self.s_delta, self.s_xnsl), outs=(self.s_wsc,))

    def core_dg0(self, g):
        return g if g < self.c.NDG else None

    def _seg_ap(self, start, n):
        c = self.c
        g0, p0 = divmod(start, 128)
        assert p0 + n <= 128, "segment may not cross col-group boundary"
        return self.s_prT.t[p0:p0 + n, g0 * c.TC:(g0 + 1) * c.TC]

    def _scan_gen(self, j):
        c, TC = self.c, self.c.TC
        per_off = (j * TC) % c.PER
        NBS = c.NBS

        def bcast(blk):
            nb0 = blk * NBS
            bbc = self.s_bbc[blk % 2]
            cbc = self.s_cbc[blk % 2]
            self.dma("ACT", bbc, self.brows[j % 2], stream="bc",
                     src_ap=self.brows[j % 2].t.ap()[nb0:nb0 + NBS, :]
                     .unsqueeze(0).to_broadcast([128, NBS, TC]),
                     dst_ap=bbc.t.ap().rearrange("p (n t) -> p n t", n=NBS))
            self.dma("ACT", cbc, self.crows, stream="bc",
                     src_ap=self.crows.t.ap()[nb0:nb0 + NBS,
                                              per_off:per_off + TC]
                     .unsqueeze(0).to_broadcast([128, NBS, TC]),
                     dst_ap=cbc.t.ap().rearrange("p (n t) -> p n t",
                                                 n=NBS)[:, :, 2:TC + 2])

        bcast(0)
        T1 = TC + 2
        for blk in range(c.NBLK):
            if blk + 1 < c.NBLK:
                bcast(blk + 1)
            nb0 = blk * NBS
            bbc = self.s_bbc[blk % 2]
            cbc = self.s_cbc[blk % 2]
            for dg in range(c.NDG):
                dcol = self.s_delta.t[:, dg * TC:(dg + 1) * TC]
                ycol = self.s_y.t[:, dg * TC:(dg + 1) * TC]
                u = self.s_u[dg % 2]
                uv = u.t.ap().rearrange("p (n t) -> p n t", n=NBS)
                self.op("POOL", lambda e, _dg=dg, _uv=uv, _b=bbc:
                        e.tensor_tensor(
                    _uv[:, :, 2:T1],
                    self.s_wsc.t.ap()[:, _dg * TC:(_dg + 1) * TC]
                    .unsqueeze(1).to_broadcast([128, NBS, TC]),
                    _b.t.ap().rearrange("p (n t) -> p n t", n=NBS),
                    ALU.mult),
                    ins=(self.s_wsc, bbc), outs=(u,))
                self.op("DVE", lambda e, _dg=dg, _nb0=nb0, _uv=uv:
                        e.tensor_copy(
                    _uv[:, :, 0:2],
                    self.s_hc.t.ap()[:, _dg * c.N + _nb0:
                                     _dg * c.N + _nb0 + NBS]
                    .unsqueeze(2).to_broadcast([128, NBS, 2])),
                    ins=(self.s_hc,), outs=(u,))
                for i in range(NBS):
                    n = nb0 + i
                    self.op("ACT", lambda e, _d=dcol, _dg=dg, _n=n, _i=i:
                            e.activation(
                                out=self.s_dA.t[:, _i * T1 + 2:
                                                (_i + 1) * T1],
                                in_=_d, func=AF.Exp,
                                scale=self.s_negA.t[
                                    :, _dg * c.N + _n:_dg * c.N + _n + 1]),
                            ins=(self.s_delta, self.s_negA),
                            outs=(self.s_dA,))
                self.op("DVE", lambda e, _u=u: e.tensor_tensor_scan(
                    out=self.s_h.t[:, :], data0=self.s_dA.t[:, :],
                    data1=_u.t[:, :], initial=0.0,
                    op0=ALU.mult, op1=ALU.add),
                    ins=(self.s_dA, u), outs=(self.s_h,))
                self.op("DVE", lambda e, _dg=dg, _nb0=nb0: e.tensor_copy(
                    self.s_hc.t[:, _dg * c.N + _nb0:_dg * c.N + _nb0 + NBS],
                    self.s_h.t.ap().rearrange("p (n t) -> p n t", n=NBS)
                    [:, :, T1 - 1]),
                    ins=(self.s_h,), outs=(self.s_hc,))
                self.op("POOL" if dg % 2 == 0 else "DVE",
                        lambda e, _u=u, _c=cbc: e.tensor_tensor(
                    _u.t[:, :], self.s_h.t[:, :], _c.t[:, :],
                    ALU.mult), ins=(self.s_h, cbc), outs=(u,))
                self.op("DVE", lambda e, _u=u: e.tensor_tensor(
                    self.s_g2.t[:, 0:2 * T1], _u.t[:, 0:2 * T1],
                    _u.t[:, 2 * T1:4 * T1], ALU.add),
                    ins=(u,), outs=(self.s_g2,))
                if blk == 0:
                    self.op("DVE", lambda e, _y=ycol: e.tensor_tensor(
                        _y, self.s_g2.t[:, 2:T1],
                        self.s_g2.t[:, T1 + 2:2 * T1], ALU.add),
                        ins=(self.s_g2,), outs=(self.s_y,))
                else:
                    self.op("DVE", lambda e: e.tensor_tensor(
                        self.s_g2.t[:, 2 * T1 + 2:2 * T1 + 2 + TC],
                        self.s_g2.t[:, 2:T1],
                        self.s_g2.t[:, T1 + 2:2 * T1], ALU.add),
                        ins=(self.s_g2,), outs=(self.s_g2,))
                    self.op("DVE", lambda e, _y=ycol: e.tensor_tensor(
                        _y, _y,
                        self.s_g2.t[:, 2 * T1 + 2:2 * T1 + 2 + TC],
                        ALU.add),
                        ins=(self.s_g2, self.s_y), outs=(self.s_y,))
                yield
        for dg in range(c.NDG):
            self.op("DVE", lambda e, _dg=dg: e.scalar_tensor_tensor(
                out=self.s_z.t[:, _dg * TC:(_dg + 1) * TC],
                in0=self.s_xbf.t[:, _dg * TC:(_dg + 1) * TC],
                scalar=self.s_dpar.t[:, _dg:_dg + 1],
                in1=self.s_y.t[:, _dg * TC:(_dg + 1) * TC],
                op0=ALU.mult, op1=ALU.add),
                ins=(self.s_xbf, self.s_dpar, self.s_y), outs=(self.s_z,))

    def _z_outproj(self, j):
        c, TC = self.c, self.c.TC
        ntg = TC // 128
        for dt in range(c.NDT):
            wo = self.s_wo2[dt % 2]
            self.dma("SYNC", wo, self.wc_wo, stream="lw",
                     src_ap=self.wc_wo.t.ap().rearrange(
                         "p (g d) -> p g d", g=c.NDG)
                     [:, :, dt * 512:(dt + 1) * 512],
                     dst_ap=wo.t.ap().rearrange("p (g d) -> p g d", g=c.NDG))
            for tg in range(ntg):
                pb = self.p_mm[tg % 2]
                for dg in range(c.NDG):
                    self.mm(lambda e, _tg=tg, _dt=dt, _dg=dg, _pb=pb,
                            _wo=wo: e.matmul(
                        _pb.t[:, 0:512],
                        self.s_z.t[:, _dg * TC + _tg * 128:_dg * TC + _tg * 128 + 128],
                        _wo.t[:, _dg * 512:(_dg + 1) * 512],
                        start=(_dg == 0), stop=(_dg == c.NDG - 1),
                        skip_group_check=True),
                        ins=(self.s_z, wo), psum_out=pb,
                        first=(dg == 0), last=(dg == c.NDG - 1),
                        note=f"op{j}.{tg}.{dt}")
                ov = self.s_opv[(dt * ntg + tg) % 2]
                self.op("ACT", lambda e, _pb=pb, _o=ov: e.activation(
                    out=_o.t[:, :], in_=_pb.t[:, 0:512], func=AF.Copy),
                    ins=(pb,), outs=(ov,))
                self.dma("ACT", self.rs1i[j % 2], ov, stream="sa",
                         dst_ap=self.rs1i[j % 2].t[
                             tg * 128:tg * 128 + 128,
                             dt * 512:(dt + 1) * 512],
                         src_ap=ov.t[:, :])

    def _rs1(self, j):
        self.coll("ReduceScatter", ALU.add,
                  [[0, 1, 2, 3], [4, 5, 6, 7]],
                  self.rs1i[j % 2], self.rs1o[j % 2], note=f"rs1.{j}")

    def _x1_stage(self, j):
        c = self.c
        mo = self.s_mo[j % 2]
        x1f = self.s_x1f
        TPC, D = c.TPC, c.D
        self.dma("SYNC", mo, self.rs1o[j % 2], stream="lo")
        self.dma("SYNC", x1f, self.x_own, stream="lo",
                 src_ap=self.x_own.t[j * TPC:(j + 1) * TPC, :])
        self.op("DVE", lambda e: e.tensor_tensor(
            mo.t[:, :], mo.t[:, :], x1f.t[:, :], ALU.add),
            ins=(mo, x1f), outs=(mo,))
        self._ln_rows(mo, self.s_st1, self.s_fdmp, x1f, None, TPC)
        self.dma("ACT", self.x1d, x1f, stream="sa",
                 dst_ap=self.x1d.t[j * TPC:(j + 1) * TPC, :])
        for dc in range(c.DG):
            self.mm(lambda e, _dc=dc: e.transpose(
                self.p_op[0].t[0:128, 0:TPC],
                x1f.t[:, _dc * 128:(_dc + 1) * 128],
                self.s_eyef.t[0:TPC, 0:TPC]),
                ins=(x1f, self.s_eyef), psum_out=self.p_op[0], first=True,
                last=True, note=f"x1tr{j}.{dc}")
            self.op("ACT", lambda e, _dc=dc: e.activation(
                out=self.s_x1fm.t[:, _dc * TPC:(_dc + 1) * TPC],
                in_=self.p_op[0].t[0:128, 0:TPC], func=AF.Copy),
                ins=(self.p_op[0],), outs=(self.s_x1fm,))
        ag = self.agi[j % 2]
        self.dma("ACT", ag, self.s_x1fm, stream="sa")

    def _ln_rows(self, src, st, dump, dst_f32, dst_bf16, rows):
        c, D = self.c, self.c.D
        self.op("ACT", lambda e: e.activation(
            out=dump.t[0:rows, :], in_=src.t[:, :], func=AF.Copy,
            accum_out=st.t[0:rows, 0:1]), ins=(src,), outs=(dump, st))
        self.op("ACT", lambda e: e.activation(
            out=dump.t[0:rows, :], in_=src.t[:, :], func=AF.Square,
            accum_out=st.t[0:rows, 1:2]), ins=(src,), outs=(dump, st))
        self.op("DVE", lambda e: e.tensor_scalar_mul(
            st.t[0:rows, 2:3], st.t[0:rows, 0:1], -1.0 / D),
            ins=(st,), outs=(st,))
        self.op("DVE", lambda e: e.tensor_scalar_mul(
            st.t[0:rows, 3:4], st.t[0:rows, 1:2], 1.0 / D),
            ins=(st,), outs=(st,))
        self.op("DVE", lambda e: e.tensor_tensor(
            st.t[0:rows, 4:5], st.t[0:rows, 2:3], st.t[0:rows, 2:3],
            ALU.mult), ins=(st,), outs=(st,))
        self.op("DVE", lambda e: e.tensor_tensor(
            st.t[0:rows, 3:4], st.t[0:rows, 3:4], st.t[0:rows, 4:5],
            ALU.subtract), ins=(st,), outs=(st,))
        self.op("ACT", lambda e: e.activation(
            out=st.t[0:rows, 3:4], in_=st.t[0:rows, 3:4], func=AF.Ln,
            bias=self.s_eps.t[0:rows, 0:1]),
            ins=(st, self.s_eps), outs=(st,))
        self.op("ACT", lambda e: e.activation(
            out=st.t[0:rows, 3:4], in_=st.t[0:rows, 3:4], func=AF.Exp,
            scale=-0.5), ins=(st,), outs=(st,))
        if dst_f32 is not None:
            self.op("DVE", lambda e: e.tensor_scalar(
                out=dst_f32.t[:, :], in0=src.t[:, :],
                scalar1=st.t[0:rows, 2:3], scalar2=st.t[0:rows, 3:4],
                op0=ALU.add, op1=ALU.mult),
                ins=(src, st), outs=(dst_f32,))
        if dst_bf16 is not None:
            self.op("DVE", lambda e: e.tensor_scalar(
                out=dst_bf16.t[0:rows, :], in0=src.t[:, :],
                scalar1=st.t[0:rows, 2:3], scalar2=st.t[0:rows, 3:4],
                op0=ALU.add, op1=ALU.mult),
                ins=(src, st), outs=(dst_bf16,))

    def _ag(self, j):
        self.coll("AllGather", ALU.bypass, [list(range(NCORES))],
                  self.agi[j % 2], self.ago[j % 2], note=f"ag.{j}")

    def _x1g_tok(self, g, tok0, ntok):
        c = self.c
        return self.s_x1g.t[:, g * c.TH + tok0:g * c.TH + tok0 + ntok]

    def _ffn_gen(self, j):
        c = self.c
        TH, DG = c.TH, c.DG
        E1 = c.E + 1
        for th in range(c.NTH):
            gt0 = th * TH
            r0 = gt0 // c.TPC
            nr = TH // c.TPC
            W = DG * c.TPC
            for r in range(r0, r0 + nr):
                self.dma("SYNC", self.s_agst, self.ago[j % 2], stream="lo",
                         src_ap=self.ago[j % 2].t[r * 128:(r + 1) * 128, :],
                         dst_ap=self.s_agst.t[:, :])
                self.op("ACT", lambda e, _r=r - r0: e.activation(
                    out=self.s_x1g.t.ap().rearrange(
                        "p (g t) -> p g t", g=DG)
                    [:, :, _r * c.TPC:(_r + 1) * c.TPC],
                    in_=self.s_agst.t.ap().rearrange(
                        "p (g t) -> p g t", g=DG),
                    func=AF.Copy),
                    ins=(self.s_agst,), outs=(self.s_x1g,))
            yield
            for tg in range(c.NTG):
                for dc in range(DG):
                    self.mm(lambda e, _dc=dc, _tg=tg: e.matmul(
                        self.p_g.t[:, 0:E1],
                        self._x1g_tok(_dc, _tg * 128, 128),
                        self.s_wr9T.t[:, _dc * E1:(_dc + 1) * E1],
                        start=(_dc == 0), stop=(_dc == DG - 1),
                        skip_group_check=True),
                        ins=(self.s_x1g, self.s_wr9T), psum_out=self.p_g,
                        first=(dc == 0), last=(dc == DG - 1),
                        note=f"rt{j}.{th}.{tg}")
                self.op("ACT", lambda e: e.activation(
                    out=self.s_lg.t[:, :], in_=self.p_g.t[:, 0:E1],
                    func=AF.Copy), ins=(self.p_g,), outs=(self.s_lg,))
                self._router_topk(j, th, tg)
                yield
            cd = self.cwrowd[j % 2]
            self.dma("ACT", cd, self.s_cwst, stream="sa",
                     dst_ap=cd.t[:, gt0:gt0 + TH],
                     src_ap=self.s_cwst.t[0:2, 0:TH])
            self.dma("ACT", self.s_cwbc, cd, stream="bc",
                     src_ap=cd.t.ap()[0:1, gt0:gt0 + TH].to_broadcast(
                         [128, TH]),
                     dst_ap=self.s_cwbc.t[:, 0:TH])
            self.dma("ACT", self.s_sgbc, cd, stream="bc",
                     src_ap=cd.t.ap()[1:2, gt0:gt0 + TH].to_broadcast(
                         [128, TH]),
                     dst_ap=self.s_sgbc.t[:, 0:TH])
            yield
            nsh = len(c.sh_tiles)
            for k, (wsrc, scol, w, dblk) in enumerate(self._gu_pairs()):
                for dstb, blk in ((self.s_wg, dblk[0]), (self.s_wu, dblk[1])):
                    self.dma("SYNC", dstb, self.wc_gu, stream="lw",
                             src_ap=self.wc_gu.t[:, blk:blk + DG * w],
                             dst_ap=dstb.t[:, 0:DG * w])
                for dc in range(DG):
                    self.mm(lambda e, _dc=dc, _w=w: e.matmul(
                        self.p_g.t[0:_w, 0:TH],
                        self.s_wg.t[:, _dc * _w:(_dc + 1) * _w],
                        self._x1g_tok(_dc, 0, TH),
                        start=(_dc == 0), stop=(_dc == DG - 1),
                        skip_group_check=True),
                        ins=(self.s_wg, self.s_x1g), psum_out=self.p_g,
                        first=(dc == 0), last=(dc == DG - 1),
                        note=f"gug{j}.{th}.{k}")
                for dc in range(DG):
                    self.mm(lambda e, _dc=dc, _w=w: e.matmul(
                        self.p_u.t[0:_w, 0:TH],
                        self.s_wu.t[:, _dc * _w:(_dc + 1) * _w],
                        self._x1g_tok(_dc, 0, TH),
                        start=(_dc == 0), stop=(_dc == DG - 1),
                        skip_group_check=True),
                        ins=(self.s_wu, self.s_x1g), psum_out=self.p_u,
                        first=(dc == 0), last=(dc == DG - 1),
                        note=f"guu{j}.{th}.{k}")
                self.op("ACT", lambda e, _w=w: e.activation(
                    out=self.s_sg.t[0:_w, :], in_=self.p_g.t[0:_w, 0:TH],
                    func=AF.Silu), ins=(self.p_g,), outs=(self.s_sg,))
                self.op("ACT", lambda e, _w=w: e.activation(
                    out=self.s_ue.t[0:_w, :], in_=self.p_u.t[0:_w, 0:TH],
                    func=AF.Copy), ins=(self.p_u,), outs=(self.s_ue,))
                mi = self.s_mid.t[:, k * TH:(k + 1) * TH]
                self.op("POOL", lambda e, _w=w, _mi=mi: e.tensor_tensor(
                    _mi[0:_w, :], self.s_sg.t[0:_w, :],
                    self.s_ue.t[0:_w, :], ALU.mult),
                    ins=(self.s_sg, self.s_ue), outs=(self.s_mid,))
                scl = self.s_sgbc if k < nsh else self.s_cwbc
                self.op("POOL", lambda e, _w=w, _mi=mi, _s=scl: e.tensor_tensor(
                    _mi[0:_w, :], _mi[0:_w, :], _s.t[0:_w, 0:TH], ALU.mult),
                    ins=(self.s_mid, scl), outs=(self.s_mid,))
                yield
            dn_banks = [self.p_g, self.p_u, self.p_o2[0], self.p_o2[1]]
            assert c.NTG <= 4
            for dt in range(c.NDT):
                roff = 0
                for k, w in enumerate(c.dn_tiles):
                    wt = self.s_wdn2[k % 2]
                    self.dma("SYNC", wt, self.wc_dn, stream="lw",
                             src_ap=self.wc_dn.t[roff:roff + w,
                                                 dt * 512:(dt + 1) * 512],
                             dst_ap=wt.t[0:w, :])
                    roff += w
                    for tg in range(c.NTG):
                        pb = dn_banks[tg]
                        self.mm(lambda e, _k=k, _w=w, _tg=tg, _pb=pb,
                                _wt=wt: e.matmul(
                            _pb.t[:, :],
                            self.s_mid.t[0:_w, _k * TH + _tg * 128:
                                          _k * TH + _tg * 128 + 128],
                            _wt.t[0:_w, :],
                            start=(_k == 0), stop=(_k == len(c.dn_tiles) - 1),
                            skip_group_check=True),
                            ins=(self.s_mid, wt), psum_out=pb,
                            first=(k == 0), last=(k == len(c.dn_tiles) - 1),
                            note=f"dn{j}.{th}.{dt}.{tg}", force_tok=True)
                for tg in range(c.NTG):
                    pb = dn_banks[tg]
                    ov = self.s_o2[tg % 2]
                    self.op("ACT", lambda e, _pb=pb, _o=ov: e.activation(
                        out=_o.t[:, :], in_=_pb.t[:, :], func=AF.Copy),
                        ins=(pb,), outs=(ov,))
                    self.dma("ACT", self.rs2i[j % 2], ov, stream="sa",
                             dst_ap=self.rs2i[j % 2].t[
                                 gt0 + tg * 128:gt0 + tg * 128 + 128,
                                 dt * 512:(dt + 1) * 512],
                             src_ap=ov.t[:, :])
                yield

    def _gu_pairs(self):
        c = self.c
        tl = self.gu_tile_starts
        nsh = len(c.sh_tiles)
        nex = len(c.ex_tiles)
        out = []
        for k in range(nsh):
            g = tl[k]
            u = tl[nsh + k]
            out.append((g[0], g[1], g[2], (g[3], u[3])))
        for k in range(nex):
            g = tl[2 * nsh + k]
            u = tl[2 * nsh + nex + k]
            out.append((g[0], g[1], g[2], (g[3], u[3])))
        return out

    def _rs2(self, j):
        self.coll("ReduceScatter", ALU.add, [list(range(NCORES))],
                  self.rs2i[j % 2], self.rs2o[j % 2], note=f"rs2.{j}")

    def _router_topk(self, j, th, tg):
        c = self.c
        E = c.E
        lg, pr, t2, tc = self.s_lg, self.s_pr, self.s_t2, self.s_tc
        self.op("DVE", lambda e: e.reduce_max(
            tc.t[:, 0:1], lg.t[:, 0:E], axis=AX.X), ins=(lg,), outs=(tc,))
        self.op("DVE", lambda e: e.tensor_scalar_mul(
            tc.t[:, 1:2], tc.t[:, 0:1], -1.0), ins=(tc,), outs=(tc,))
        self.op("ACT", lambda e: e.activation(
            out=pr.t[:, :], in_=lg.t[:, 0:E], func=AF.Exp,
            bias=tc.t[:, 1:2], accum_out=tc.t[:, 2:3]),
            ins=(lg, tc), outs=(pr, tc))
        self.op("DVE", lambda e: e.reciprocal(
            tc.t[:, 3:4], tc.t[:, 2:3]), ins=(tc,), outs=(tc,))
        self.op("DVE", lambda e: e.tensor_scalar_mul(
            pr.t[:, :], pr.t[:, :], tc.t[:, 3:4]), ins=(pr, tc), outs=(pr,))
        self.op("DVE", lambda e: e.reduce_max(
            self.s_mrv.t[:, 0:1], pr.t[:, :], axis=AX.X),
            ins=(pr,), outs=(self.s_mrv,))
        self.op("DVE", lambda e: e.match_replace(
            out=t2.t[:, 0:E], in_to_replace=self.s_mrv.t[:, 0:8],
            in_values=pr.t[:, :], imm_value=-1.0),
            ins=(self.s_mrv, pr), outs=(t2,))
        self.op("DVE", lambda e: e.reduce_max(
            tc.t[:, 5:6], t2.t[:, 0:E], axis=AX.X), ins=(t2,), outs=(tc,))
        self.op("DVE", lambda e: e.tensor_scalar(
            out=t2.t[:, E:2 * E], in0=pr.t[:, :], scalar1=tc.t[:, 5:6],
            scalar2=None, op0=ALU.is_ge), ins=(pr, tc), outs=(t2,))
        self.op("DVE", lambda e: e.tensor_tensor(
            t2.t[:, 0:E], t2.t[:, E:2 * E], pr.t[:, :], ALU.mult),
            ins=(t2, pr), outs=(t2,))
        self.op("DVE", lambda e: e.tensor_tensor(
            t2.t[:, E:2 * E], t2.t[:, 0:E], self.s_esel.t[:, :], ALU.mult),
            ins=(t2, self.s_esel), outs=(t2,))
        self.op("DVE", lambda e: e.reduce_sum(
            self.s_cwsg.t[:, 0:1], t2.t[:, E:2 * E], axis=AX.X),
            ins=(t2,), outs=(self.s_cwsg,))
        self.op("ACT", lambda e: e.activation(
            out=tc.t[:, 6:7], in_=lg.t[:, E:E + 1],
            func=AF.Exp, scale=-1.0), ins=(lg,), outs=(tc,))
        self.op("DVE", lambda e: e.tensor_scalar_add(
            tc.t[:, 6:7], tc.t[:, 6:7], 1.0), ins=(tc,), outs=(tc,))
        self.op("DVE", lambda e: e.reciprocal(
            self.s_cwsg.t[:, 1:2], tc.t[:, 6:7]),
            ins=(tc,), outs=(self.s_cwsg,))
        self.mm(lambda e: e.transpose(
            self.p_op[1].t[0:2, 0:128], self.s_cwsg.t[:, :],
            self.s_eyef.t[:, :]),
            ins=(self.s_cwsg, self.s_eyef), psum_out=self.p_op[1], first=True,
            last=True, note=f"cwtr{j}.{th}.{tg}")
        self.op("ACT", lambda e, _tg=tg: e.activation(
            out=self.s_cwst.t[0:2, _tg * 128:_tg * 128 + 128],
            in_=self.p_op[1].t[0:2, 0:128], func=AF.Copy),
            ins=(self.p_op[1],), outs=(self.s_cwst,))

    def _final(self, j):
        c = self.c
        TPC = c.TPC
        self.dma("SYNC", self.s_fin, self.rs2o[j % 2], stream="lo")
        self.dma("SYNC", self.s_mo[0], self.x1d, stream="lo",
                 src_ap=self.x1d.t[j * TPC:(j + 1) * TPC, :])
        self.op("DVE", lambda e: e.tensor_tensor(
            self.s_fin.t[:, :], self.s_fin.t[:, :],
            self.s_mo[0].t[:, :], ALU.add),
            ins=(self.s_fin, self.s_mo[0]), outs=(self.s_fin,))
        self._ln_rows(self.s_fin, self.s_fst, self.s_fdmp, self.s_fin, None,
                      TPC)
        tok = self.dma("ACT", self.out, self.s_fin, stream="so",
                       dst_ap=self.out.t[j * TPC:(j + 1) * TPC, :])
        self.fin_toks.append(tok)

    def _replay(self):
        nc = self.nc
        with contextlib.ExitStack() as es2:
            sems = {}
            for key in sorted(set(list(self.s.cnt.keys()) +
                                  [k for q in QUEUES
                                   for it in self.s.items[q]
                                   for k, v in it[0]])):
                sems[key] = es2.enter_context(
                    nc.semaphore(key.replace(".", "_")))
            block = es2.enter_context(nc.Block())
            decs = {"PE": block.tensor, "DVE": block.vector,
                    "ACT": block.scalar, "POOL": block.gpsimd,
                    "SYNC": block.sync}

            def mk(items):
                def body(eng):
                    for waits, fn, key, inc, note in items:
                        for dk, dv in waits:
                            eng.wait_ge(sems[dk], dv)
                        if fn is None:
                            continue
                        inst = fn(eng)
                        if inc:
                            inst.then_inc(sems[key], inc)
                return body

            for q in QUEUES:
                decs[q](mk(self.s.items[q]))
        self.es.close()
        return nc



def _pp(v):
    return np.ascontiguousarray(v.reshape(-1, 128).T)


def make_inputs(cfg, core, L0_attn_x, L0_attn_norm_w, L0_attn_norm_b,
                L0_attn_x_proj_w, L0_attn_dt_proj_w, L0_attn_dt_proj_b,
                L0_attn_A_log, L0_attn_D_param, L0_attn_out_proj_w,
                L0_ffn_W_router, L0_ffn_W_sgate, L0_ffn_Ws_gate_up,
                L0_ffn_Ws_down, L0_ffn_We_gate_up, L0_ffn_We_down):
    c = cfg
    b, g = divmod(core, GRP)
    f32 = np.float32

    def C(a):
        return np.ascontiguousarray(a, dtype=f32)

    sl = np.r_[np.arange(g * c.DSL, (g + 1) * c.DSL),
               np.arange(0, g * c.DSL),
               np.arange((g + 1) * c.DSL, c.D)]
    x = np.asarray(L0_attn_x)[b]
    x_fm = C(x.T[sl])
    tok = np.concatenate([np.arange(j * c.TC + g * c.TPC,
                                    j * c.TC + (g + 1) * c.TPC)
                          for j in range(c.NCH)])
    x_own = C(x[tok])
    nw = np.asarray(L0_attn_norm_w)[sl]
    nb = np.asarray(L0_attn_norm_b)[sl]
    xpwT = C(np.asarray(L0_attn_x_proj_w).T[sl])
    dsl = np.arange(g * c.DSL, (g + 1) * c.DSL)
    dtwT = C(np.asarray(L0_attn_dt_proj_w)[dsl].T)
    dtb = _pp(np.asarray(L0_attn_dt_proj_b)[dsl])
    alog = C(np.asarray(L0_attn_A_log)[dsl].reshape(c.NDG, 128, c.N)
             .transpose(1, 0, 2).reshape(128, c.NDG * c.N))
    dpar = _pp(np.asarray(L0_attn_D_param)[dsl])
    woutT = C(np.asarray(L0_attn_out_proj_w).T[dsl])
    wr9 = np.concatenate([np.asarray(L0_ffn_W_router),
                          np.asarray(L0_ffn_W_sgate)], axis=0)
    wr9T = C(wr9.T)
    esel = np.zeros((128, c.E), f32)
    esel[:, core] = 1.0
    eye = np.eye(128, dtype=f32)
    s0 = core * c.ISS
    wsg = np.asarray(L0_ffn_Ws_gate_up)
    ws_gu = C(np.concatenate([wsg[:, s0:s0 + c.ISS],
                              wsg[:, c.ISH + s0:c.ISH + s0 + c.ISS]], axis=1))
    ws_dn = C(np.asarray(L0_ffn_Ws_down)[s0:s0 + c.ISS])
    we_gu = C(np.asarray(L0_ffn_We_gate_up)[core])
    we_dn = C(np.asarray(L0_ffn_We_down)[core])
    return {
        "x_fm": x_fm, "x_own": x_own,
        "ln1w": C(_pp(nw)), "ln1b": C(_pp(nb)),
        "xpwT": xpwT, "dtwT": dtwT, "dtb": C(dtb), "alog": alog,
        "dpar": C(dpar), "woutT": woutT, "wr9T": wr9T, "esel": esel,
        "eye": eye, "ws_gu": ws_gu, "ws_dn": ws_dn, "we_gu": we_gu,
        "we_dn": we_dn,
    }


def assemble(cfg, results):
    c = cfg
    out = np.empty((c.B, c.T, c.D), np.float32)
    for core in range(NCORES):
        b, g = divmod(core, GRP)
        r = results[core]["out"]
        for j in range(c.NCH):
            out[b, j * c.TC + g * c.TPC:j * c.TC + (g + 1) * c.TPC] = \
                r[j * c.TPC:(j + 1) * c.TPC]
    return out


_CFG = Cfg()


def kernel(**inputs):
    c = _CFG
    bld = Builder(c, cc_inc=1)
    nc = bld.build()
    in_maps = [make_inputs(c, core, **inputs) for core in range(NCORES)]
    res = run_bass_kernel_spmd(nc, in_maps, core_ids=list(range(NCORES)))
    return assemble(c, res.results)


# version 15
# speedup vs baseline: 1.1693x; 1.1693x over previous
import contextlib
from dataclasses import dataclass

import numpy as np
import concourse.bass as bass
import concourse.mybir as mybir
from concourse.bass_utils import run_bass_kernel_spmd

F32 = mybir.dt.float32
BF16 = mybir.dt.bfloat16
AF = mybir.ActivationFunctionType
ALU = mybir.AluOpType
AX = mybir.AxisListType

NCORES = 8
GRP = 4


def _tiles(total, t=128):
    out = []
    while total > 0:
        out.append(min(t, total))
        total -= out[-1]
    return out


@dataclass
class Cfg:
    B: int = 2
    T: int = 4096
    D: int = 2048
    N: int = 64
    R: int = 128
    PER: int = 512
    E: int = 8
    ISH: int = 5632
    IMOE: int = 1408
    NCH: int = 8
    eps: float = 1e-5

    def __post_init__(self):
        c = self
        assert c.D % (GRP * 128) == 0
        c.DSL = c.D // GRP
        c.NDG = c.DSL // 128
        c.DG = c.D // 128
        c.TC = c.T // c.NCH
        assert c.TC % GRP == 0 and c.TC <= 512
        c.TPC = c.TC // GRP
        assert c.TPC <= 128
        c.GT = 2 * c.TC
        assert c.N % 8 == 0
        c.NBS = 4
        assert c.N % c.NBS == 0
        c.NBLK = c.N // c.NBS
        assert c.ISH % NCORES == 0
        c.ISS = c.ISH // NCORES
        c.PER = min(c.PER, c.T)
        assert c.PER % c.TC == 0
        c.CG = _tiles(c.R + 2 * c.N)
        c.NCG = len(c.CG)
        assert c.R <= 128
        c.sh_tiles = _tiles(c.ISS)
        c.ex_tiles = _tiles(c.IMOE)
        c.dn_tiles = c.sh_tiles + c.ex_tiles
        c.IT = c.ISS + c.IMOE
        c.NTH = max(1, c.GT // 512)
        c.TH = c.GT // c.NTH
        assert c.TH <= 512
        assert c.GT % 128 == 0
        c.NTG = c.TH // 128
        c.NDT = c.D // 512


QUEUES = ("PE", "DVE", "ACT", "POOL", "SYNC")
SERIALIZE = {"DVE", "ACT", "POOL"}


DMA_W = 4
CC_W = 2


class Sched:
    def __init__(self, cc_inc=1):
        self.items = {q: [] for q in QUEUES}
        self.cnt = {}
        self.hwm = {q: {} for q in QUEUES}
        self.cc_inc = cc_inc
        self.didx = {}

    def emit(self, q, kind, fn, deps=(), inc=None, note="", stream=None):
        deps = [d for d in deps if d is not None]
        if kind == "cc":
            idx = self.didx.get("cc", 0)
            self.didx["cc"] = idx + 1
            slot = idx % CC_W
            key = f"cc.{slot}"
            inc = self.cc_inc if inc is None else inc
            if idx >= CC_W:
                deps.append((key, inc * (idx // CC_W)))
            self.cnt[key] = inc * (idx // CC_W + 1)
            tok = (key, self.cnt[key])
        elif kind == "d":
            st = stream or q
            idx = self.didx.get(st, 0)
            self.didx[st] = idx + 1
            slot = idx % DMA_W
            key = f"dma.{st}.{slot}"
            inc = 16
            if idx >= DMA_W:
                deps.append((key, 16 * (idx // DMA_W)))
            self.cnt[key] = 16 * (idx // DMA_W + 1)
            tok = (key, self.cnt[key])
        else:
            key = f"{q}.{kind}"
            if inc is None:
                inc = 1
            if q in SERIALIZE:
                if self.cnt.get(key, 0) > 0:
                    deps.append((key, self.cnt[key]))
            tok = None
            if inc:
                self.cnt[key] = self.cnt.get(key, 0) + inc
                tok = (key, self.cnt[key])
        hw = self.hwm[q]
        wmap = {}
        for (dk, dv) in deps:
            if dv > hw.get(dk, 0):
                wmap[dk] = max(wmap.get(dk, 0), dv)
        for dk, dv in wmap.items():
            hw[dk] = dv
        self.items[q].append((sorted(wmap.items()), fn,
                              key if (inc or 0) else None, inc, note))
        return tok

    def wait_only(self, q, deps, note=""):
        self.emit(q, "c", None, deps, inc=0, note=note)


class Buf:
    __slots__ = ("t", "w", "r")

    def __init__(self, t):
        self.t = t
        self.w = []
        self.r = []

    def rd(self):
        return list(self.w)

    def wd(self):
        return list(self.w) + list(self.r)

    def read_by(self, tok):
        if tok is not None:
            self.r.append(tok)
            if len(self.r) > 64:
                del self.r[:32]

    def written_by(self, tok):
        if tok is not None:
            self.w.append(tok)
            if len(self.w) > 64:
                del self.w[:32]
            self.r = []


class Builder:
    def __init__(self, cfg: Cfg, cc_inc=1):
        self.c = cfg
        self.nc = bass.Bass("TRN2", target_bir_lowering=False)
        self.s = Sched(cc_inc)
        self.es = contextlib.ExitStack()
        self._pend = {}

    def sb(self, name, shape, dtype):
        return Buf(self.es.enter_context(self.nc.sbuf_tensor(name, shape, dtype)))

    def psum(self, name, shape, dtype=F32):
        return Buf(self.es.enter_context(self.nc.psum_tensor(name, shape, dtype)))

    def dram(self, name, shape, dtype, kind="Internal"):
        return Buf(self.nc.dram_tensor(name, shape, dtype, kind=kind))

    def op(self, q, fn, ins=(), outs=(), extra=(), note=""):
        deps = list(extra)
        for b in ins:
            deps += b.rd()
        for b in outs:
            deps += b.wd()
        tok = self.s.emit(q, "c", fn, deps, note=note)
        for b in ins:
            b.read_by(tok)
        for b in outs:
            b.written_by(tok)
        return tok

    def mm(self, fn, ins=(), psum_out=None, first=False, last=False,
           extra=(), note="", force_tok=False):
        deps = list(extra)
        for b in ins:
            deps += b.rd()
        if first and psum_out is not None:
            deps += psum_out.wd()
        tok = self.s.emit("PE", "c", fn, deps,
                          inc=(1 if (last or force_tok) else 0), note=note)
        if force_tok and not last:
            for b in ins:
                b.read_by(tok)
        if psum_out is not None:
            pend = self._pend.setdefault(id(psum_out), [])
            pend.extend(ins)
            if last:
                for b in {id(x): x for x in pend}.values():
                    b.read_by(tok)
                pend.clear()
                psum_out.written_by(tok)
        else:
            for b in ins:
                b.read_by(tok)
        return tok

    def dma(self, q, dst, src, dst_ap=None, src_ap=None, extra=(), note="",
            stream=None, **kw):
        deps = list(extra) + src.rd() + dst.wd()
        d_ap = dst_ap if dst_ap is not None else dst.t.ap()
        s_ap = src_ap if src_ap is not None else src.t.ap()

        def fn(e, _d=d_ap, _s=s_ap, _kw=kw):
            return e.dma_start(out=_d, in_=_s, **_kw)

        tok = self.s.emit(q, "d", fn, deps, note=note, stream=stream)
        src.read_by(tok)
        dst.written_by(tok)
        return tok

    def coll(self, kind, alu, groups, src, dst, extra=(), note=""):
        deps = list(extra) + src.rd() + dst.wd()

        def fn(e, _k=kind, _a=alu, _g=groups, _s=src.t, _d=dst.t):
            return e.collective_compute(
                _k, _a, replica_groups=_g,
                ins=[_s.ap().opt()], outs=[_d.ap().opt()])

        tok = self.s.emit("POOL", "cc", fn, deps, note=note)
        src.read_by(tok)
        dst.written_by(tok)
        return tok

    DMA_STREAMS = ("ls", "sw", "lx", "lo", "lw", "sa", "bc", "so")

    def build(self):
        c = self.c
        nc = self.nc
        dram, sb, psum = self.dram, self.sb, self.psum

        self.x_fm = dram("x_fm", [c.D, c.T], F32, "ExternalInput")
        self.x_own = dram("x_own", [c.NCH * c.TPC, c.D], F32, "ExternalInput")
        self.ln1w = dram("ln1w", [128, c.DG], F32, "ExternalInput")
        self.ln1b = dram("ln1b", [128, c.DG], F32, "ExternalInput")
        self.xpwT = dram("xpwT", [c.D, c.R + 2 * c.N], F32, "ExternalInput")
        self.dtwT = dram("dtwT", [c.R, c.DSL], F32, "ExternalInput")
        self.dtb = dram("dtb", [128, c.NDG], F32, "ExternalInput")
        self.alog = dram("alog", [128, c.NDG * c.N], F32, "ExternalInput")
        self.dpar = dram("dpar", [128, c.NDG], F32, "ExternalInput")
        self.woutT = dram("woutT", [c.DSL, c.D], F32, "ExternalInput")
        self.wr9T = dram("wr9T", [c.D, c.E + 1], F32, "ExternalInput")
        self.esel = dram("esel", [128, c.E], F32, "ExternalInput")
        self.eye = dram("eye", [128, 128], F32, "ExternalInput")
        self.ws_gu = dram("ws_gu", [c.D, 2 * c.ISS], F32, "ExternalInput")
        self.ws_dn = dram("ws_dn", [c.ISS, c.D], F32, "ExternalInput")
        self.we_gu = dram("we_gu", [c.D, 2 * c.IMOE], F32, "ExternalInput")
        self.we_dn = dram("we_dn", [c.IMOE, c.D], F32, "ExternalInput")
        self.out = dram("out", [c.NCH * c.TPC, c.D], F32, "ExternalOutput")

        self.wc_gu = dram("wc_gu", [128, c.DG * 2 * c.IT], BF16)
        self.wc_dn = dram("wc_dn", [c.IT, c.D], BF16)
        self.wc_wo = dram("wc_wo", [128, c.NDG * c.D], BF16)
        self.brows = [dram(f"brows{i}", [c.N, c.TC], BF16) for i in range(2)]
        self.crows = dram("crows", [c.N, c.PER], BF16)
        self.rowbc = [dram(f"rowbc{i}", [1, 2 * c.TC], BF16) for i in range(2)]
        self.rs1i = [dram(f"rs1i{i}", [c.TC, c.D], BF16) for i in range(2)]
        self.rs1o = [dram(f"rs1o{i}", [c.TPC, c.D], BF16) for i in range(2)]
        self.agi = [dram(f"agi{i}", [128, c.DG * c.TPC], BF16) for i in range(2)]
        self.ago = [Buf(self.nc.dram_tensor(f"ago{i}", [NCORES * 128, c.DG * c.TPC], BF16, kind="Internal", addr_space="Shared")) for i in range(2)]
        self.cwrowd = [dram(f"cwrowd{i}", [2, c.GT], BF16) for i in range(2)]
        self.rs2i = [dram(f"rs2i{i}", [c.GT, c.D], BF16) for i in range(2)]
        self.rs2o = [dram(f"rs2o{i}", [c.TPC, c.D], BF16) for i in range(2)]
        self.x1d = dram("x1d", [c.NCH * c.TPC, c.D], F32)

        self.s_eyef = sb("s_eyef", [128, 128], F32)
        self.s_ones = sb("s_ones", [128, 1], BF16)
        self.s_eps = sb("s_eps", [128, 1], F32)
        self.s_ln1w = sb("s_ln1w", [128, c.DG], F32)
        self.s_ln1wn = sb("s_ln1wn", [128, c.DG], F32)
        self.s_ln1b = sb("s_ln1b", [128, c.DG], F32)
        self.s_dtb = sb("s_dtb", [128, c.NDG], F32)
        self.s_negA = sb("s_negA", [128, c.NDG * c.N], F32)
        self.s_dpar = sb("s_dpar", [128, c.NDG], F32)
        self.s_esel = sb("s_esel", [128, c.E], F32)
        self.s_xpwT = sb("s_xpwT", [128, c.DG * (c.R + 2 * c.N)], BF16)
        self.s_dtwT = sb("s_dtwT", [c.R, c.DSL], BF16)
        self.s_wr9T = sb("s_wr9T", [128, c.DG * (c.E + 1)], BF16)

        self.s_xg = [sb(f"s_xg{i}", [128, c.TC], F32) for i in range(2)]
        self.s_xbf = sb("s_xbf", [128, c.DG * c.TC], BF16)
        self.s_sqb = [sb(f"s_sqb{i}", [128, c.TC], BF16) for i in range(2)]
        self.s_xn = [sb(f"s_xn{i}", [128, c.TC], BF16) for i in range(2)]
        self.s_xnsl = sb("s_xnsl", [128, c.NDG * c.TC], BF16)
        self.s_row = sb("s_row", [1, 2 * c.TC], F32)
        self.s_rowb = sb("s_rowb", [1, 2 * c.TC], BF16)
        self.s_rbc = sb("s_rbc", [128, 2 * c.TC], BF16)
        self.s_prT = sb("s_prT", [128, c.NCG * c.TC], BF16)
        self.s_delta = sb("s_delta", [128, c.NDG * c.TC], BF16)
        self.s_wsc = sb("s_wsc", [128, c.NDG * c.TC], BF16)
        self.s_dA = sb("s_dA", [128, c.NBS * (c.TC + 2)], F32)
        self.s_u = [sb(f"s_u{i}", [128, c.NBS * (c.TC + 2)], BF16) for i in range(2)]
        self.s_h = sb("s_h", [128, c.NBS * (c.TC + 2)], BF16)
        self.s_g = self.s_u
        self.s_g2 = self.s_h
        self.s_y = sb("s_y", [128, c.NDG * c.TC], F32)
        self.s_hc = sb("s_hc", [128, c.NDG * c.N], F32)
        self.s_bbc = [sb(f"s_bbc{i}", [128, c.NBS * c.TC], BF16) for i in range(2)]
        self.s_cbc = [sb(f"s_cbc{i}", [128, c.NBS * (c.TC + 2)], BF16) for i in range(2)]
        self.s_z = sb("s_z", [128, c.NDG * c.TC], BF16)
        self.s_opv = [sb(f"s_opv{i}", [128, 512], BF16) for i in range(2)]
        self.s_wo2 = [sb(f"s_wo2{i}", [128, c.NDG * 512], BF16) for i in range(2)]

        self.s_mo = [sb("s_mo0", [c.TPC, c.D], F32)] * 2
        self.s_x1f = sb("s_x1f", [c.TPC, c.D], F32)
        self.s_x1b = sb("s_x1b", [c.TPC, c.D], BF16)
        self.s_st1 = sb("s_st1", [c.TPC, 8], F32)
        self.s_x1fm = sb("s_x1fm", [128, c.DG * c.TPC], BF16)
        self.s_agst = self.s_x1fm

        self.s_x1g = sb("s_x1g", [128, c.DG * c.TH], BF16)
        self.s_wg = sb("s_wg", [128, c.DG * 128], BF16)
        self.s_wu = sb("s_wu", [128, c.DG * 128], BF16)
        self.s_mid = sb("s_mid", [128, len(c.dn_tiles) * c.TH], BF16)
        self.s_sg = sb("s_sg", [128, c.TH], BF16)
        self.s_ue = sb("s_ue", [128, c.TH], BF16)
        self.s_wdn2 = [sb(f"s_wdn{i}", [128, 512], BF16) for i in range(2)]
        self.s_o2 = [sb(f"s_o2{i}", [128, 512], BF16) for i in range(2)]
        self.s_lg = sb("s_lg", [128, c.E + 1], F32)
        self.s_pr = sb("s_pr", [128, c.E], F32)
        self.s_t2 = sb("s_t2", [128, 2 * c.E], F32)
        self.s_tc = sb("s_tc", [128, 8], F32)
        s_cwsg_decl = self.s_cwsg = sb("s_cwsg", [128, 2], F32)
        self.s_mrv = sb("s_mrv", [128, 8], F32)
        self.s_cwst = sb("s_cwst", [2, c.GT], BF16)
        self.s_cwbc = sb("s_cwbc", [128, c.GT], BF16)
        self.s_sgbc = sb("s_sgbc", [128, c.GT], BF16)

        self.s_fin = sb("s_fin", [c.TPC, c.D], F32)
        if c.TPC == 128:
            self.s_stg = self.s_mo[0]
            self.s_stg2 = self.s_fin
        else:
            self.s_stg = sb("s_stg", [128, max(512, c.D)], F32)
            self.s_stg2 = sb("s_stg2", [128, max(512, c.D)], F32)
        self.s_stgb = self.s_x1fm
        self.s_stgb2 = self.s_mid
        self.s_fst = sb("s_fst", [c.TPC, 8], F32)
        self.s_fdmp = self.s_x1b

        self.p_mm = [psum(f"p_mm{i}", [128, 512]) for i in range(2)]
        self.p_op = [psum(f"p_op{i}", [128, 512]) for i in range(2)]
        self.p_g = psum("p_g", [128, 512])
        self.p_u = psum("p_u", [128, 512])
        self.p_o2 = [psum(f"p_o2{i}", [128, 512]) for i in range(2)]

        self.fin_toks = []
        self._setup()
        DONE = object()
        wg = self._wcache_gen()
        while getattr(self, "_wc_phase", 0) < 1:
            if next(wg, DONE) is DONE:
                wg = None
                break
        for j in range(c.NCH + 5):
            jz, jx, jf, jo = j - 1, j - 2, j - 3, j - 4
            if 0 <= jz < c.NCH:
                self._z_outproj(jz)
                self._rs1(jz)
            if j < c.NCH:
                self._mamba_pre(j)
            sg = self._scan_gen(j) if j < c.NCH else None
            if 0 <= jf < c.NCH and wg is not None:
                while next(wg, DONE) is not DONE:
                    pass
                wg = None
            fg = self._ffn_gen(jf) if 0 <= jf < c.NCH else None
            if sg is not None:
                next(sg, DONE)
            if 0 <= jx < c.NCH:
                self._x1_stage(jx)
                self._ag(jx)
            if 0 <= jo < c.NCH:
                self._final(jo)
            while sg is not None or fg is not None:
                if sg is not None and next(sg, DONE) is DONE:
                    sg = None
                if wg is not None and next(wg, DONE) is DONE:
                    wg = None
                if fg is not None and next(fg, DONE) is DONE:
                    fg = None
            if 0 <= jf < c.NCH:
                self._rs2(jf)
            if j >= c.NCH and wg is not None:
                while next(wg, DONE) is not DONE:
                    pass
                wg = None
        self.s.wait_only("SYNC", self.fin_toks, note="output completion")
        self._replay()
        return nc

    def _cast_via(self, q, dstbuf, dst_ap, srcbuf, src_ap, note=""):
        def fn(e, _d=dst_ap, _s=src_ap):
            if q == "ACT":
                return e.activation(out=_d, in_=_s, func=AF.Copy)
            return e.tensor_copy(_d, _s)
        return self.op(q, fn, ins=(srcbuf,), outs=(dstbuf,), note=note)

    def _setup(self):
        c, s = self.cfg if False else self.c, self
        b = None
        QW = "SYNC"
        loads = [(self.s_ln1w, self.ln1w), (self.s_ln1b, self.ln1b),
                 (self.s_dtb, self.dtb), (self.s_dpar, self.dpar),
                 (self.s_esel, self.esel), (self.s_eyef, self.eye)]
        for dstb, srcb in loads:
            self.dma("SYNC", dstb, srcb, stream="ls")
        self.op("DVE", lambda e: e.memset(self.s_ones.t[:, :], 1.0),
                outs=(self.s_ones,))
        self.op("DVE", lambda e: e.memset(self.s_hc.t[:, :], 0.0),
                outs=(self.s_hc,))
        self.op("DVE", lambda e: e.memset(self.s_eps.t[:, :], float(self.c.eps)),
                outs=(self.s_eps,))
        self.op("DVE", lambda e: e.memset(self.s_mrv.t[:, :], -1.0),
                outs=(self.s_mrv,))
        self.op("DVE", lambda e: e.memset(self.s_dA.t[:, :], 0.0),
                outs=(self.s_dA,))
        for cb in self.s_cbc:
            self.op("DVE", lambda e, _cb=cb: e.memset(_cb.t[:, :], 0.0),
                    outs=(cb,))
        self.op("DVE", lambda e: e.tensor_scalar_mul(
            self.s_ln1wn.t[:, :], self.s_ln1w.t[:, :], -1.0),
            ins=(self.s_ln1w,), outs=(self.s_ln1wn,))
        self.dma("SYNC", self.s_stg2, self.alog, stream="ls",
                 dst_ap=self.s_stg2.t[:, 0:c.NDG * c.N])
        self.op("ACT", lambda e: e.activation(
            out=self.s_negA.t[:, :], in_=self.s_stg2.t[:, 0:c.NDG * c.N],
            func=AF.Exp), ins=(self.s_stg2,), outs=(self.s_negA,))
        self.op("DVE", lambda e: e.tensor_scalar_mul(
            self.s_negA.t[:, :], self.s_negA.t[:, :], -1.0),
            ins=(self.s_negA,), outs=(self.s_negA,))


    def _wcache_gen(self):
        c = self.c
        self._wc_phase = 0
        pairs = [(self.s_stg, self.s_stgb), (self.s_stg2, self.s_stgb2)]
        pair_cols = [min(p[0].t.ap().shape[1], p[1].t.ap().shape[1])
                     for p in pairs]
        self._pi = 0

        self._wc_count = 0

        def cast_piece(srcb, src_ap_fn, width, stores):
            pi = self._pi
            self._pi += 1
            self._wc_count += 1
            stg, stgb = pairs[pi % 2]
            self.dma("SYNC", stg, srcb, stream="ls",
                     dst_ap=stg.t[0:src_ap_fn.__rows__, 0:width],
                     src_ap=src_ap_fn(width))
            self._cast_via("ACT", stgb,
                           stgb.t[0:src_ap_fn.__rows__, 0:width],
                           stg, stg.t[0:src_ap_fn.__rows__, 0:width])
            for dstb, dst_ap, col0, w in stores:
                self.dma("ACT", dstb, stgb, stream="sw", dst_ap=dst_ap,
                         src_ap=stgb.t[0:src_ap_fn.__rows__,
                                       col0:col0 + w])

        def srcfn(srcb, r0, r1, c0):
            def f(width):
                return srcb.t[r0:r1, c0:c0 + width]
            f.__rows__ = r1 - r0
            return f

        CGA = c.R + 2 * c.N
        for g in range(c.DG):
            cast_piece(self.xpwT, srcfn(self.xpwT, g * 128, (g + 1) * 128, 0),
                       CGA, [(self.s_xpwT,
                              self.s_xpwT.t[:, g * CGA:(g + 1) * CGA], 0,
                              CGA)])
            if self._wc_count % 4 == 0:
                yield
        cast_piece(self.dtwT, srcfn(self.dtwT, 0, c.R, 0), c.DSL,
                   [(self.s_dtwT, self.s_dtwT.t[:, :], 0, c.DSL)])
        E1 = c.E + 1
        for g in range(c.DG):
            cast_piece(self.wr9T, srcfn(self.wr9T, g * 128, (g + 1) * 128, 0),
                       E1, [(self.s_wr9T,
                             self.s_wr9T.t[:, g * E1:(g + 1) * E1], 0, E1)])
        for g in range(c.NDG):
            col = 0
            while col < c.D:
                w = min(min(pair_cols), c.D - col)
                cast_piece(self.woutT,
                           srcfn(self.woutT, g * 128, (g + 1) * 128, col), w,
                           [(self.wc_wo,
                             self.wc_wo.t[:, g * c.D + col:g * c.D + col + w],
                             0, w)])
                col += w
                if self._wc_count % 4 == 0:
                    yield
        self._wc_phase = 1
        yield
        secs = []
        blk0 = 0
        sec_defs = [(self.ws_gu, 0, c.sh_tiles), (self.ws_gu, c.ISS, c.sh_tiles),
                    (self.we_gu, 0, c.ex_tiles), (self.we_gu, c.IMOE, c.ex_tiles)]
        tile_list = []
        for srcb, soff, tiles in sec_defs:
            off = 0
            for w in tiles:
                tile_list.append((srcb, soff + off, w, blk0))
                off += w
                blk0 += c.DG * w
        self.gu_tile_starts = tile_list
        for g in range(c.DG):
            for srcb, soff, tiles in sec_defs:
                width = sum(tiles)
                col = 0
                while col < width:
                    w = min(min(pair_cols), width - col)
                    stores = []
                    toff = 0
                    for (tsrc, tscol, tw, tblk) in tile_list:
                        if tsrc is srcb and tscol >= soff and \
                                tscol < soff + width:
                            rel = tscol - soff
                            if rel >= col and rel + tw <= col + w:
                                stores.append((self.wc_gu,
                                               self.wc_gu.t[:, tblk + g * tw:
                                                            tblk + (g + 1) * tw],
                                               rel - col, tw))
                    exp_tiles = sum(1 for (tsrc, tscol, tw, _tb) in tile_list
                                    if tsrc is srcb and soff <= tscol <
                                    soff + width and col <= tscol - soff and
                                    tscol - soff + tw <= col + w)
                    assert len(stores) == exp_tiles and stores, \
                        (g, soff, col, w)
                    cast_piece(srcb,
                               srcfn(srcb, g * 128, (g + 1) * 128, soff + col),
                               w, stores)
                    col += w
                    if self._wc_count % 4 == 0:
                        yield
        row = 0
        for srcb, nrows in ((self.ws_dn, c.ISS), (self.we_dn, c.IMOE)):
            done = 0
            while done < nrows:
                r = min(128, nrows - done)
                col = 0
                while col < c.D:
                    w = min(min(pair_cols), c.D - col)
                    cast_piece(srcb, srcfn(srcb, done, done + r, col), w,
                               [(self.wc_dn,
                                 self.wc_dn.t[row:row + r, col:col + w],
                                 0, w)])
                    col += w
                    if self._wc_count % 4 == 0:
                        yield
                done += r
                row += r


    def _mamba_pre(self, j):
        c = self.c
        t0 = j * c.TC
        TC = c.TC
        for g in range(c.DG):
            xg = self.s_xg[g % 2]
            sq = self.s_sqb[g % 2]
            xb = self.s_xbf.t[:, g * TC:(g + 1) * TC]
            self.dma("SYNC", xg, self.x_fm, stream="lx",
                     src_ap=self.x_fm.t[g * 128:(g + 1) * 128, t0:t0 + c.TC])
            self.op("ACT", lambda e, _x=xg, _xb=xb: e.activation(
                out=_xb, in_=_x.t[:, :], func=AF.Copy),
                ins=(xg,), outs=(self.s_xbf,))
            self.op("ACT", lambda e, _s=sq, _x=xg: e.activation(
                out=_s.t[:, :], in_=_x.t[:, :], func=AF.Square),
                ins=(xg,), outs=(sq,))
            self.mm(lambda e, _xb=xb, _g=g: e.matmul(
                self.p_mm[0].t[0:1, 0:c.TC], self.s_ones.t[:, :], _xb,
                start=(_g == 0), stop=(_g == c.DG - 1),
                skip_group_check=True),
                ins=(self.s_xbf, self.s_ones), psum_out=self.p_mm[0],
                first=(g == 0), last=(g == c.DG - 1), note=f"sum{j}",
                force_tok=True)
            self.mm(lambda e, _s=sq, _g=g: e.matmul(
                self.p_mm[1].t[0:1, 0:c.TC], self.s_ones.t[:, :], _s.t[:, :],
                start=(_g == 0), stop=(_g == c.DG - 1),
                skip_group_check=True),
                ins=(sq, self.s_ones), psum_out=self.p_mm[1],
                first=(g == 0), last=(g == c.DG - 1), note=f"sq{j}",
                force_tok=True)
        rowt = self.s_row.t
        self.op("DVE", lambda e: e.tensor_scalar_mul(
            rowt[0:1, 0:TC], self.p_mm[0].t[0:1, 0:TC], 1.0 / c.D),
            ins=(self.p_mm[0],), outs=(self.s_row,))
        self.op("DVE", lambda e: e.tensor_scalar_mul(
            rowt[0:1, TC:2 * TC], self.p_mm[1].t[0:1, 0:TC], 1.0 / c.D),
            ins=(self.p_mm[1],), outs=(self.s_row,))
        self.op("DVE", lambda e: e.tensor_tensor(
            self.p_mm[0].t[0:1, 0:TC], rowt[0:1, 0:TC], rowt[0:1, 0:TC],
            ALU.mult), ins=(self.s_row,), outs=(self.p_mm[0],))
        self.op("DVE", lambda e: e.tensor_tensor(
            rowt[0:1, TC:2 * TC], rowt[0:1, TC:2 * TC],
            self.p_mm[0].t[0:1, 0:TC], ALU.subtract),
            ins=(self.s_row, self.p_mm[0]), outs=(self.s_row,))
        self.op("ACT", lambda e: e.activation(
            out=rowt[0:1, TC:2 * TC], in_=rowt[0:1, TC:2 * TC], func=AF.Ln,
            bias=self.s_eps.t[0:1, 0:1]),
            ins=(self.s_row, self.s_eps), outs=(self.s_row,))
        self.op("ACT", lambda e: e.activation(
            out=rowt[0:1, TC:2 * TC], in_=rowt[0:1, TC:2 * TC], func=AF.Exp,
            scale=-0.5), ins=(self.s_row,), outs=(self.s_row,))
        self.op("DVE", lambda e: e.tensor_tensor(
            rowt[0:1, 0:TC], rowt[0:1, 0:TC], rowt[0:1, TC:2 * TC], ALU.mult),
            ins=(self.s_row,), outs=(self.s_row,))
        self.op("DVE", lambda e: e.tensor_copy(
            self.s_rowb.t[:, :], rowt[0:1, 0:2 * TC]),
            ins=(self.s_row,), outs=(self.s_rowb,))
        rb = self.rowbc[j % 2]
        self.dma("ACT", rb, self.s_rowb, stream="sa",
                 src_ap=self.s_rowb.t[0:1, :])
        self.dma("ACT", self.s_rbc, rb, stream="bc",
                 src_ap=rb.t.ap()[0:1, 0:TC].to_broadcast([128, TC]),
                 dst_ap=self.s_rbc.t[:, 0:TC])
        self.dma("ACT", self.s_rbc, rb, stream="bc",
                 src_ap=rb.t.ap()[0:1, TC:2 * TC].to_broadcast([128, TC]),
                 dst_ap=self.s_rbc.t[:, TC:2 * TC])

        CGA = c.R + 2 * c.N
        for g in range(c.DG):
            xn = self.s_xn[g % 2]
            t1 = self.s_sqb[g % 2]
            xb = self.s_xbf.t[:, g * TC:(g + 1) * TC]
            self.op("DVE", lambda e, _xb=xb, _g=g, _t=t1:
                    e.scalar_tensor_tensor(
                out=_t.t[:, :], in0=_xb,
                scalar=self.s_ln1w.t[:, _g:_g + 1],
                in1=self.s_rbc.t[:, TC:2 * TC], op0=ALU.mult, op1=ALU.mult),
                ins=(self.s_xbf, self.s_ln1w, self.s_rbc), outs=(t1,))
            self.op("DVE", lambda e, _g=g, _t=t1, _xn=xn:
                    e.scalar_tensor_tensor(
                out=_xn.t[:, :], in0=self.s_rbc.t[:, 0:TC],
                scalar=self.s_ln1wn.t[:, _g:_g + 1],
                in1=_t.t[:, :], op0=ALU.mult, op1=ALU.add),
                ins=(self.s_rbc, self.s_ln1wn, t1), outs=(xn,))
            self.op("DVE", lambda e, _xn=xn, _g=g: e.tensor_scalar(
                out=_xn.t[:, :], in0=_xn.t[:, :],
                scalar1=self.s_ln1b.t[:, _g:_g + 1], scalar2=None,
                op0=ALU.add), ins=(xn, self.s_ln1b), outs=(xn,))
            if g < c.NDG:
                self.op("DVE", lambda e, _xn=xn, _d=g: e.tensor_copy(
                    self.s_xnsl.t[:, _d * TC:(_d + 1) * TC], _xn.t[:, :]),
                    ins=(xn,), outs=(self.s_xnsl,))
            co = 0
            for ci, cw in enumerate(c.CG):
                self.mm(lambda e, _xn=xn, _g=g, _ci=ci, _co=co, _cw=cw:
                        e.matmul(
                            self.p_mm[_ci % 2].t[0:_cw, 0:c.TC],
                            self.s_xpwT.t[:, _g * CGA + _co:_g * CGA + _co + _cw],
                            _xn.t[:, :],
                            start=(_g == 0), stop=(_g == c.DG - 1),
                            skip_group_check=True),
                        ins=(xn, self.s_xpwT), psum_out=self.p_mm[ci % 2],
                        first=(g == 0), last=(g == c.DG - 1),
                        note=f"proj{j}.{ci}", force_tok=True)
                co += cw
        for ci, cw in enumerate(c.CG):
            self.op("ACT", lambda e, _ci=ci, _cw=cw: e.activation(
                out=self.s_prT.t[0:_cw, _ci * TC:(_ci + 1) * TC],
                in_=self.p_mm[_ci % 2].t[0:_cw, 0:TC], func=AF.Copy),
                ins=(self.p_mm[ci % 2],), outs=(self.s_prT,))
        bseg = self._seg_ap(c.R, c.N)
        cseg = self._seg_ap(c.R + c.N, c.N)
        self.dma("ACT", self.brows[j % 2], self.s_prT, stream="sa",
                 src_ap=bseg)
        if (j * c.TC) < c.PER:
            self.dma("ACT", self.crows, self.s_prT, stream="sa", src_ap=cseg,
                     dst_ap=self.crows.t[:, j * c.TC:(j + 1) * c.TC])
        for dg in range(c.NDG):
            pm = self.p_mm[dg % 2]
            self.mm(lambda e, _dg=dg, _pm=pm: e.matmul(
                _pm.t[:, 0:c.TC], self.s_dtwT.t[:, _dg * 128:(_dg + 1) * 128],
                self.s_prT.t[0:c.R, 0:TC], start=True, stop=True,
                skip_group_check=True),
                ins=(self.s_dtwT, self.s_prT), psum_out=pm, first=True,
                last=True, note=f"dt{j}.{dg}")
            dcol = self.s_delta.t[:, dg * TC:(dg + 1) * TC]
            sa = self.s_sqb[0]
            sl = self.s_sqb[1]
            self.op("DVE", lambda e, _pm=pm, _dg=dg, _d=dcol: e.tensor_scalar(
                out=_d, in0=_pm.t[:, 0:TC],
                scalar1=self.s_dtb.t[:, _dg:_dg + 1], scalar2=None,
                op0=ALU.add),
                ins=(pm, self.s_dtb), outs=(self.s_delta,))
            self.op("ACT", lambda e, _d=dcol, _sa=sa: e.activation(
                out=_sa.t[:, :], in_=_d, func=AF.Abs),
                ins=(self.s_delta,), outs=(sa,))
            self.op("ACT", lambda e, _sa=sa, _sl=sl: e.activation(
                out=_sl.t[:, :], in_=_sa.t[:, :],
                func=AF.Exp, scale=-1.0), ins=(sa,), outs=(sl,))
            self.op("ACT", lambda e, _sl=sl: e.activation(
                out=_sl.t[:, :], in_=_sl.t[:, :],
                func=AF.Ln, bias=1.0), ins=(sl,), outs=(sl,))
            self.op("DVE", lambda e, _d=dcol: e.tensor_scalar_max(_d, _d, 0.0),
                    ins=(self.s_delta,), outs=(self.s_delta,))
            self.op("DVE", lambda e, _d=dcol, _sl=sl: e.tensor_tensor(
                _d, _d, _sl.t[:, :], ALU.add),
                ins=(self.s_delta, sl), outs=(self.s_delta,))
            self.op("POOL", lambda e, _d=dcol, _dg=dg: e.tensor_tensor(
                self.s_wsc.t[:, _dg * TC:(_dg + 1) * TC], _d,
                self.s_xnsl.t[:, _dg * TC:(_dg + 1) * TC], ALU.mult),
                ins=(self.s_delta, self.s_xnsl), outs=(self.s_wsc,))

    def core_dg0(self, g):
        return g if g < self.c.NDG else None

    def _seg_ap(self, start, n):
        c = self.c
        g0, p0 = divmod(start, 128)
        assert p0 + n <= 128, "segment may not cross col-group boundary"
        return self.s_prT.t[p0:p0 + n, g0 * c.TC:(g0 + 1) * c.TC]

    def _scan_gen(self, j):
        c, TC = self.c, self.c.TC
        per_off = (j * TC) % c.PER
        NBS = c.NBS

        def bcast(blk):
            nb0 = blk * NBS
            bbc = self.s_bbc[blk % 2]
            cbc = self.s_cbc[blk % 2]
            self.dma("ACT", bbc, self.brows[j % 2], stream="bc",
                     src_ap=self.brows[j % 2].t.ap()[nb0:nb0 + NBS, :]
                     .unsqueeze(0).to_broadcast([128, NBS, TC]),
                     dst_ap=bbc.t.ap().rearrange("p (n t) -> p n t", n=NBS))
            self.dma("ACT", cbc, self.crows, stream="bc",
                     src_ap=self.crows.t.ap()[nb0:nb0 + NBS,
                                              per_off:per_off + TC]
                     .unsqueeze(0).to_broadcast([128, NBS, TC]),
                     dst_ap=cbc.t.ap().rearrange("p (n t) -> p n t",
                                                 n=NBS)[:, :, 2:TC + 2])

        bcast(0)
        T1 = TC + 2
        for blk in range(c.NBLK):
            if blk + 1 < c.NBLK:
                bcast(blk + 1)
            nb0 = blk * NBS
            bbc = self.s_bbc[blk % 2]
            cbc = self.s_cbc[blk % 2]
            for dg in range(c.NDG):
                dcol = self.s_delta.t[:, dg * TC:(dg + 1) * TC]
                ycol = self.s_y.t[:, dg * TC:(dg + 1) * TC]
                u = self.s_u[dg % 2]
                uv = u.t.ap().rearrange("p (n t) -> p n t", n=NBS)
                self.op("POOL", lambda e, _dg=dg, _uv=uv, _b=bbc:
                        e.tensor_tensor(
                    _uv[:, :, 2:T1],
                    self.s_wsc.t.ap()[:, _dg * TC:(_dg + 1) * TC]
                    .unsqueeze(1).to_broadcast([128, NBS, TC]),
                    _b.t.ap().rearrange("p (n t) -> p n t", n=NBS),
                    ALU.mult),
                    ins=(self.s_wsc, bbc), outs=(u,))
                self.op("DVE", lambda e, _dg=dg, _nb0=nb0, _uv=uv:
                        e.tensor_copy(
                    _uv[:, :, 0:2],
                    self.s_hc.t.ap()[:, _dg * c.N + _nb0:
                                     _dg * c.N + _nb0 + NBS]
                    .unsqueeze(2).to_broadcast([128, NBS, 2])),
                    ins=(self.s_hc,), outs=(u,))
                for i in range(NBS):
                    n = nb0 + i
                    self.op("ACT", lambda e, _d=dcol, _dg=dg, _n=n, _i=i:
                            e.activation(
                                out=self.s_dA.t[:, _i * T1 + 2:
                                                (_i + 1) * T1],
                                in_=_d, func=AF.Exp,
                                scale=self.s_negA.t[
                                    :, _dg * c.N + _n:_dg * c.N + _n + 1]),
                            ins=(self.s_delta, self.s_negA),
                            outs=(self.s_dA,))
                self.op("DVE", lambda e, _u=u: e.tensor_tensor_scan(
                    out=self.s_h.t[:, :], data0=self.s_dA.t[:, :],
                    data1=_u.t[:, :], initial=0.0,
                    op0=ALU.mult, op1=ALU.add),
                    ins=(self.s_dA, u), outs=(self.s_h,))
                self.op("DVE", lambda e, _dg=dg, _nb0=nb0: e.tensor_copy(
                    self.s_hc.t[:, _dg * c.N + _nb0:_dg * c.N + _nb0 + NBS],
                    self.s_h.t.ap().rearrange("p (n t) -> p n t", n=NBS)
                    [:, :, T1 - 1]),
                    ins=(self.s_h,), outs=(self.s_hc,))
                self.op("POOL" if dg % 2 == 0 else "DVE",
                        lambda e, _u=u, _c=cbc: e.tensor_tensor(
                    _u.t[:, :], self.s_h.t[:, :], _c.t[:, :],
                    ALU.mult), ins=(self.s_h, cbc), outs=(u,))
                self.op("DVE", lambda e, _u=u: e.tensor_tensor(
                    self.s_g2.t[:, 0:2 * T1], _u.t[:, 0:2 * T1],
                    _u.t[:, 2 * T1:4 * T1], ALU.add),
                    ins=(u,), outs=(self.s_g2,))
                if blk == 0:
                    self.op("DVE", lambda e, _y=ycol: e.tensor_tensor(
                        _y, self.s_g2.t[:, 2:T1],
                        self.s_g2.t[:, T1 + 2:2 * T1], ALU.add),
                        ins=(self.s_g2,), outs=(self.s_y,))
                else:
                    self.op("DVE", lambda e: e.tensor_tensor(
                        self.s_g2.t[:, 2 * T1 + 2:2 * T1 + 2 + TC],
                        self.s_g2.t[:, 2:T1],
                        self.s_g2.t[:, T1 + 2:2 * T1], ALU.add),
                        ins=(self.s_g2,), outs=(self.s_g2,))
                    self.op("DVE", lambda e, _y=ycol: e.tensor_tensor(
                        _y, _y,
                        self.s_g2.t[:, 2 * T1 + 2:2 * T1 + 2 + TC],
                        ALU.add),
                        ins=(self.s_g2, self.s_y), outs=(self.s_y,))
                yield
        for dg in range(c.NDG):
            self.op("DVE", lambda e, _dg=dg: e.scalar_tensor_tensor(
                out=self.s_z.t[:, _dg * TC:(_dg + 1) * TC],
                in0=self.s_xbf.t[:, _dg * TC:(_dg + 1) * TC],
                scalar=self.s_dpar.t[:, _dg:_dg + 1],
                in1=self.s_y.t[:, _dg * TC:(_dg + 1) * TC],
                op0=ALU.mult, op1=ALU.add),
                ins=(self.s_xbf, self.s_dpar, self.s_y), outs=(self.s_z,))

    def _z_outproj(self, j):
        c, TC = self.c, self.c.TC
        ntg = TC // 128
        for dt in range(c.NDT):
            wo = self.s_wo2[dt % 2]
            self.dma("SYNC", wo, self.wc_wo, stream="lw",
                     src_ap=self.wc_wo.t.ap().rearrange(
                         "p (g d) -> p g d", g=c.NDG)
                     [:, :, dt * 512:(dt + 1) * 512],
                     dst_ap=wo.t.ap().rearrange("p (g d) -> p g d", g=c.NDG))
            for tg in range(ntg):
                pb = self.p_mm[tg % 2]
                for dg in range(c.NDG):
                    self.mm(lambda e, _tg=tg, _dt=dt, _dg=dg, _pb=pb,
                            _wo=wo: e.matmul(
                        _pb.t[:, 0:512],
                        self.s_z.t[:, _dg * TC + _tg * 128:_dg * TC + _tg * 128 + 128],
                        _wo.t[:, _dg * 512:(_dg + 1) * 512],
                        start=(_dg == 0), stop=(_dg == c.NDG - 1),
                        skip_group_check=True),
                        ins=(self.s_z, wo), psum_out=pb,
                        first=(dg == 0), last=(dg == c.NDG - 1),
                        note=f"op{j}.{tg}.{dt}")
                ov = self.s_opv[(dt * ntg + tg) % 2]
                self.op("ACT", lambda e, _pb=pb, _o=ov: e.activation(
                    out=_o.t[:, :], in_=_pb.t[:, 0:512], func=AF.Copy),
                    ins=(pb,), outs=(ov,))
                self.dma("ACT", self.rs1i[j % 2], ov, stream="sa",
                         dst_ap=self.rs1i[j % 2].t[
                             tg * 128:tg * 128 + 128,
                             dt * 512:(dt + 1) * 512],
                         src_ap=ov.t[:, :])

    def _rs1(self, j):
        self.coll("ReduceScatter", ALU.add,
                  [[0, 1, 2, 3], [4, 5, 6, 7]],
                  self.rs1i[j % 2], self.rs1o[j % 2], note=f"rs1.{j}")

    def _x1_stage(self, j):
        c = self.c
        mo = self.s_mo[j % 2]
        x1f = self.s_x1f
        TPC, D = c.TPC, c.D
        self.dma("SYNC", self.s_x1b, self.rs1o[j % 2], stream="lo",
                 dst_ap=self.s_x1b.t[0:TPC, :])
        self.dma("SYNC", x1f, self.x_own, stream="lo",
                 src_ap=self.x_own.t[j * TPC:(j + 1) * TPC, :])
        self.op("DVE", lambda e: e.tensor_tensor(
            mo.t[:, :], self.s_x1b.t[0:TPC, :], x1f.t[:, :], ALU.add),
            ins=(self.s_x1b, x1f), outs=(mo,))
        self._ln_rows(mo, self.s_st1, self.s_fdmp, x1f, None, TPC)
        self.dma("ACT", self.x1d, x1f, stream="sa",
                 dst_ap=self.x1d.t[j * TPC:(j + 1) * TPC, :])
        for dc in range(c.DG):
            self.mm(lambda e, _dc=dc: e.transpose(
                self.p_op[0].t[0:128, 0:TPC],
                x1f.t[:, _dc * 128:(_dc + 1) * 128],
                self.s_eyef.t[0:TPC, 0:TPC]),
                ins=(x1f, self.s_eyef), psum_out=self.p_op[0], first=True,
                last=True, note=f"x1tr{j}.{dc}")
            self.op("ACT", lambda e, _dc=dc: e.activation(
                out=self.s_x1fm.t[:, _dc * TPC:(_dc + 1) * TPC],
                in_=self.p_op[0].t[0:128, 0:TPC], func=AF.Copy),
                ins=(self.p_op[0],), outs=(self.s_x1fm,))
        ag = self.agi[j % 2]
        self.dma("ACT", ag, self.s_x1fm, stream="sa")

    def _ln_rows(self, src, st, dump, dst_f32, dst_bf16, rows):
        c, D = self.c, self.c.D
        self.op("ACT", lambda e: e.activation(
            out=dump.t[0:rows, :], in_=src.t[:, :], func=AF.Copy,
            accum_out=st.t[0:rows, 0:1]), ins=(src,), outs=(dump, st))
        self.op("ACT", lambda e: e.activation(
            out=dump.t[0:rows, :], in_=src.t[:, :], func=AF.Square,
            accum_out=st.t[0:rows, 1:2]), ins=(src,), outs=(dump, st))
        self.op("DVE", lambda e: e.tensor_scalar_mul(
            st.t[0:rows, 2:3], st.t[0:rows, 0:1], -1.0 / D),
            ins=(st,), outs=(st,))
        self.op("DVE", lambda e: e.tensor_scalar_mul(
            st.t[0:rows, 3:4], st.t[0:rows, 1:2], 1.0 / D),
            ins=(st,), outs=(st,))
        self.op("DVE", lambda e: e.tensor_tensor(
            st.t[0:rows, 4:5], st.t[0:rows, 2:3], st.t[0:rows, 2:3],
            ALU.mult), ins=(st,), outs=(st,))
        self.op("DVE", lambda e: e.tensor_tensor(
            st.t[0:rows, 3:4], st.t[0:rows, 3:4], st.t[0:rows, 4:5],
            ALU.subtract), ins=(st,), outs=(st,))
        self.op("ACT", lambda e: e.activation(
            out=st.t[0:rows, 3:4], in_=st.t[0:rows, 3:4], func=AF.Ln,
            bias=self.s_eps.t[0:rows, 0:1]),
            ins=(st, self.s_eps), outs=(st,))
        self.op("ACT", lambda e: e.activation(
            out=st.t[0:rows, 3:4], in_=st.t[0:rows, 3:4], func=AF.Exp,
            scale=-0.5), ins=(st,), outs=(st,))
        if dst_f32 is not None:
            self.op("DVE", lambda e: e.tensor_scalar(
                out=dst_f32.t[:, :], in0=src.t[:, :],
                scalar1=st.t[0:rows, 2:3], scalar2=st.t[0:rows, 3:4],
                op0=ALU.add, op1=ALU.mult),
                ins=(src, st), outs=(dst_f32,))
        if dst_bf16 is not None:
            self.op("DVE", lambda e: e.tensor_scalar(
                out=dst_bf16.t[0:rows, :], in0=src.t[:, :],
                scalar1=st.t[0:rows, 2:3], scalar2=st.t[0:rows, 3:4],
                op0=ALU.add, op1=ALU.mult),
                ins=(src, st), outs=(dst_bf16,))

    def _ag(self, j):
        self.coll("AllGather", ALU.bypass, [list(range(NCORES))],
                  self.agi[j % 2], self.ago[j % 2], note=f"ag.{j}")

    def _x1g_tok(self, g, tok0, ntok):
        c = self.c
        return self.s_x1g.t[:, g * c.TH + tok0:g * c.TH + tok0 + ntok]

    def _ffn_gen(self, j):
        c = self.c
        TH, DG = c.TH, c.DG
        E1 = c.E + 1
        for th in range(c.NTH):
            gt0 = th * TH
            r0 = gt0 // c.TPC
            nr = TH // c.TPC
            W = DG * c.TPC
            for r in range(r0, r0 + nr):
                self.dma("SYNC", self.s_agst, self.ago[j % 2], stream="lo",
                         src_ap=self.ago[j % 2].t[r * 128:(r + 1) * 128, :],
                         dst_ap=self.s_agst.t[:, :])
                self.op("ACT", lambda e, _r=r - r0: e.activation(
                    out=self.s_x1g.t.ap().rearrange(
                        "p (g t) -> p g t", g=DG)
                    [:, :, _r * c.TPC:(_r + 1) * c.TPC],
                    in_=self.s_agst.t.ap().rearrange(
                        "p (g t) -> p g t", g=DG),
                    func=AF.Copy),
                    ins=(self.s_agst,), outs=(self.s_x1g,))
            yield
            for tg in range(c.NTG):
                for dc in range(DG):
                    self.mm(lambda e, _dc=dc, _tg=tg: e.matmul(
                        self.p_g.t[:, 0:E1],
                        self._x1g_tok(_dc, _tg * 128, 128),
                        self.s_wr9T.t[:, _dc * E1:(_dc + 1) * E1],
                        start=(_dc == 0), stop=(_dc == DG - 1),
                        skip_group_check=True),
                        ins=(self.s_x1g, self.s_wr9T), psum_out=self.p_g,
                        first=(dc == 0), last=(dc == DG - 1),
                        note=f"rt{j}.{th}.{tg}")
                self.op("ACT", lambda e: e.activation(
                    out=self.s_lg.t[:, :], in_=self.p_g.t[:, 0:E1],
                    func=AF.Copy), ins=(self.p_g,), outs=(self.s_lg,))
                self._router_topk(j, th, tg)
                yield
            cd = self.cwrowd[j % 2]
            self.dma("ACT", cd, self.s_cwst, stream="sa",
                     dst_ap=cd.t[:, gt0:gt0 + TH],
                     src_ap=self.s_cwst.t[0:2, 0:TH])
            self.dma("ACT", self.s_cwbc, cd, stream="bc",
                     src_ap=cd.t.ap()[0:1, gt0:gt0 + TH].to_broadcast(
                         [128, TH]),
                     dst_ap=self.s_cwbc.t[:, 0:TH])
            self.dma("ACT", self.s_sgbc, cd, stream="bc",
                     src_ap=cd.t.ap()[1:2, gt0:gt0 + TH].to_broadcast(
                         [128, TH]),
                     dst_ap=self.s_sgbc.t[:, 0:TH])
            yield
            nsh = len(c.sh_tiles)
            for k, (wsrc, scol, w, dblk) in enumerate(self._gu_pairs()):
                for dstb, blk in ((self.s_wg, dblk[0]), (self.s_wu, dblk[1])):
                    self.dma("SYNC", dstb, self.wc_gu, stream="lw",
                             src_ap=self.wc_gu.t[:, blk:blk + DG * w],
                             dst_ap=dstb.t[:, 0:DG * w])
                for dc in range(DG):
                    self.mm(lambda e, _dc=dc, _w=w: e.matmul(
                        self.p_g.t[0:_w, 0:TH],
                        self.s_wg.t[:, _dc * _w:(_dc + 1) * _w],
                        self._x1g_tok(_dc, 0, TH),
                        start=(_dc == 0), stop=(_dc == DG - 1),
                        skip_group_check=True),
                        ins=(self.s_wg, self.s_x1g), psum_out=self.p_g,
                        first=(dc == 0), last=(dc == DG - 1),
                        note=f"gug{j}.{th}.{k}")
                for dc in range(DG):
                    self.mm(lambda e, _dc=dc, _w=w: e.matmul(
                        self.p_u.t[0:_w, 0:TH],
                        self.s_wu.t[:, _dc * _w:(_dc + 1) * _w],
                        self._x1g_tok(_dc, 0, TH),
                        start=(_dc == 0), stop=(_dc == DG - 1),
                        skip_group_check=True),
                        ins=(self.s_wu, self.s_x1g), psum_out=self.p_u,
                        first=(dc == 0), last=(dc == DG - 1),
                        note=f"guu{j}.{th}.{k}")
                self.op("ACT", lambda e, _w=w: e.activation(
                    out=self.s_sg.t[0:_w, :], in_=self.p_g.t[0:_w, 0:TH],
                    func=AF.Silu), ins=(self.p_g,), outs=(self.s_sg,))
                self.op("ACT", lambda e, _w=w: e.activation(
                    out=self.s_ue.t[0:_w, :], in_=self.p_u.t[0:_w, 0:TH],
                    func=AF.Copy), ins=(self.p_u,), outs=(self.s_ue,))
                mi = self.s_mid.t[:, k * TH:(k + 1) * TH]
                self.op("POOL", lambda e, _w=w, _mi=mi: e.tensor_tensor(
                    _mi[0:_w, :], self.s_sg.t[0:_w, :],
                    self.s_ue.t[0:_w, :], ALU.mult),
                    ins=(self.s_sg, self.s_ue), outs=(self.s_mid,))
                scl = self.s_sgbc if k < nsh else self.s_cwbc
                self.op("POOL", lambda e, _w=w, _mi=mi, _s=scl: e.tensor_tensor(
                    _mi[0:_w, :], _mi[0:_w, :], _s.t[0:_w, 0:TH], ALU.mult),
                    ins=(self.s_mid, scl), outs=(self.s_mid,))
                yield
            dn_banks = [self.p_g, self.p_u, self.p_o2[0], self.p_o2[1]]
            assert c.NTG <= 4
            for dt in range(c.NDT):
                roff = 0
                for k, w in enumerate(c.dn_tiles):
                    wt = self.s_wdn2[k % 2]
                    self.dma("SYNC", wt, self.wc_dn, stream="lw",
                             src_ap=self.wc_dn.t[roff:roff + w,
                                                 dt * 512:(dt + 1) * 512],
                             dst_ap=wt.t[0:w, :])
                    roff += w
                    for tg in range(c.NTG):
                        pb = dn_banks[tg]
                        self.mm(lambda e, _k=k, _w=w, _tg=tg, _pb=pb,
                                _wt=wt: e.matmul(
                            _pb.t[:, :],
                            self.s_mid.t[0:_w, _k * TH + _tg * 128:
                                          _k * TH + _tg * 128 + 128],
                            _wt.t[0:_w, :],
                            start=(_k == 0), stop=(_k == len(c.dn_tiles) - 1),
                            skip_group_check=True),
                            ins=(self.s_mid, wt), psum_out=pb,
                            first=(k == 0), last=(k == len(c.dn_tiles) - 1),
                            note=f"dn{j}.{th}.{dt}.{tg}", force_tok=True)
                for tg in range(c.NTG):
                    pb = dn_banks[tg]
                    ov = self.s_o2[tg % 2]
                    self.op("ACT", lambda e, _pb=pb, _o=ov: e.activation(
                        out=_o.t[:, :], in_=_pb.t[:, :], func=AF.Copy),
                        ins=(pb,), outs=(ov,))
                    self.dma("ACT", self.rs2i[j % 2], ov, stream="sa",
                             dst_ap=self.rs2i[j % 2].t[
                                 gt0 + tg * 128:gt0 + tg * 128 + 128,
                                 dt * 512:(dt + 1) * 512],
                             src_ap=ov.t[:, :])
                yield

    def _gu_pairs(self):
        c = self.c
        tl = self.gu_tile_starts
        nsh = len(c.sh_tiles)
        nex = len(c.ex_tiles)
        out = []
        for k in range(nsh):
            g = tl[k]
            u = tl[nsh + k]
            out.append((g[0], g[1], g[2], (g[3], u[3])))
        for k in range(nex):
            g = tl[2 * nsh + k]
            u = tl[2 * nsh + nex + k]
            out.append((g[0], g[1], g[2], (g[3], u[3])))
        return out

    def _rs2(self, j):
        self.coll("ReduceScatter", ALU.add, [list(range(NCORES))],
                  self.rs2i[j % 2], self.rs2o[j % 2], note=f"rs2.{j}")

    def _router_topk(self, j, th, tg):
        c = self.c
        E = c.E
        lg, pr, t2, tc = self.s_lg, self.s_pr, self.s_t2, self.s_tc
        self.op("DVE", lambda e: e.reduce_max(
            tc.t[:, 0:1], lg.t[:, 0:E], axis=AX.X), ins=(lg,), outs=(tc,))
        self.op("DVE", lambda e: e.tensor_scalar_mul(
            tc.t[:, 1:2], tc.t[:, 0:1], -1.0), ins=(tc,), outs=(tc,))
        self.op("ACT", lambda e: e.activation(
            out=pr.t[:, :], in_=lg.t[:, 0:E], func=AF.Exp,
            bias=tc.t[:, 1:2], accum_out=tc.t[:, 2:3]),
            ins=(lg, tc), outs=(pr, tc))
        self.op("DVE", lambda e: e.reciprocal(
            tc.t[:, 3:4], tc.t[:, 2:3]), ins=(tc,), outs=(tc,))
        self.op("DVE", lambda e: e.tensor_scalar_mul(
            pr.t[:, :], pr.t[:, :], tc.t[:, 3:4]), ins=(pr, tc), outs=(pr,))
        self.op("DVE", lambda e: e.reduce_max(
            self.s_mrv.t[:, 0:1], pr.t[:, :], axis=AX.X),
            ins=(pr,), outs=(self.s_mrv,))
        self.op("DVE", lambda e: e.match_replace(
            out=t2.t[:, 0:E], in_to_replace=self.s_mrv.t[:, 0:8],
            in_values=pr.t[:, :], imm_value=-1.0),
            ins=(self.s_mrv, pr), outs=(t2,))
        self.op("DVE", lambda e: e.reduce_max(
            tc.t[:, 5:6], t2.t[:, 0:E], axis=AX.X), ins=(t2,), outs=(tc,))
        self.op("DVE", lambda e: e.tensor_scalar(
            out=t2.t[:, E:2 * E], in0=pr.t[:, :], scalar1=tc.t[:, 5:6],
            scalar2=None, op0=ALU.is_ge), ins=(pr, tc), outs=(t2,))
        self.op("DVE", lambda e: e.tensor_tensor(
            t2.t[:, 0:E], t2.t[:, E:2 * E], pr.t[:, :], ALU.mult),
            ins=(t2, pr), outs=(t2,))
        self.op("DVE", lambda e: e.tensor_tensor(
            t2.t[:, E:2 * E], t2.t[:, 0:E], self.s_esel.t[:, :], ALU.mult),
            ins=(t2, self.s_esel), outs=(t2,))
        self.op("DVE", lambda e: e.reduce_sum(
            self.s_cwsg.t[:, 0:1], t2.t[:, E:2 * E], axis=AX.X),
            ins=(t2,), outs=(self.s_cwsg,))
        self.op("ACT", lambda e: e.activation(
            out=tc.t[:, 6:7], in_=lg.t[:, E:E + 1],
            func=AF.Exp, scale=-1.0), ins=(lg,), outs=(tc,))
        self.op("DVE", lambda e: e.tensor_scalar_add(
            tc.t[:, 6:7], tc.t[:, 6:7], 1.0), ins=(tc,), outs=(tc,))
        self.op("DVE", lambda e: e.reciprocal(
            self.s_cwsg.t[:, 1:2], tc.t[:, 6:7]),
            ins=(tc,), outs=(self.s_cwsg,))
        self.mm(lambda e: e.transpose(
            self.p_op[1].t[0:2, 0:128], self.s_cwsg.t[:, :],
            self.s_eyef.t[:, :]),
            ins=(self.s_cwsg, self.s_eyef), psum_out=self.p_op[1], first=True,
            last=True, note=f"cwtr{j}.{th}.{tg}")
        self.op("ACT", lambda e, _tg=tg: e.activation(
            out=self.s_cwst.t[0:2, _tg * 128:_tg * 128 + 128],
            in_=self.p_op[1].t[0:2, 0:128], func=AF.Copy),
            ins=(self.p_op[1],), outs=(self.s_cwst,))

    def _final(self, j):
        c = self.c
        TPC = c.TPC
        self.dma("SYNC", self.s_x1b, self.rs2o[j % 2], stream="lo",
                 dst_ap=self.s_x1b.t[0:TPC, :])
        self.dma("SYNC", self.s_mo[0], self.x1d, stream="lo",
                 src_ap=self.x1d.t[j * TPC:(j + 1) * TPC, :])
        self.op("DVE", lambda e: e.tensor_tensor(
            self.s_fin.t[:, :], self.s_x1b.t[0:TPC, :],
            self.s_mo[0].t[:, :], ALU.add),
            ins=(self.s_x1b, self.s_mo[0]), outs=(self.s_fin,))
        self._ln_rows(self.s_fin, self.s_fst, self.s_fdmp, self.s_fin, None,
                      TPC)
        tok = self.dma("ACT", self.out, self.s_fin, stream="so",
                       dst_ap=self.out.t[j * TPC:(j + 1) * TPC, :])
        self.fin_toks.append(tok)

    def _replay(self):
        nc = self.nc
        with contextlib.ExitStack() as es2:
            sems = {}
            for key in sorted(set(list(self.s.cnt.keys()) +
                                  [k for q in QUEUES
                                   for it in self.s.items[q]
                                   for k, v in it[0]])):
                sems[key] = es2.enter_context(
                    nc.semaphore(key.replace(".", "_")))
            block = es2.enter_context(nc.Block())
            decs = {"PE": block.tensor, "DVE": block.vector,
                    "ACT": block.scalar, "POOL": block.gpsimd,
                    "SYNC": block.sync}

            def mk(items):
                def body(eng):
                    for waits, fn, key, inc, note in items:
                        for dk, dv in waits:
                            eng.wait_ge(sems[dk], dv)
                        if fn is None:
                            continue
                        inst = fn(eng)
                        if inc:
                            inst.then_inc(sems[key], inc)
                return body

            for q in QUEUES:
                decs[q](mk(self.s.items[q]))
        self.es.close()
        return nc



def _pp(v):
    return np.ascontiguousarray(v.reshape(-1, 128).T)


def make_inputs(cfg, core, L0_attn_x, L0_attn_norm_w, L0_attn_norm_b,
                L0_attn_x_proj_w, L0_attn_dt_proj_w, L0_attn_dt_proj_b,
                L0_attn_A_log, L0_attn_D_param, L0_attn_out_proj_w,
                L0_ffn_W_router, L0_ffn_W_sgate, L0_ffn_Ws_gate_up,
                L0_ffn_Ws_down, L0_ffn_We_gate_up, L0_ffn_We_down):
    c = cfg
    b, g = divmod(core, GRP)
    f32 = np.float32

    def C(a):
        return np.ascontiguousarray(a, dtype=f32)

    sl = np.r_[np.arange(g * c.DSL, (g + 1) * c.DSL),
               np.arange(0, g * c.DSL),
               np.arange((g + 1) * c.DSL, c.D)]
    x = np.asarray(L0_attn_x)[b]
    x_fm = C(x.T[sl])
    tok = np.concatenate([np.arange(j * c.TC + g * c.TPC,
                                    j * c.TC + (g + 1) * c.TPC)
                          for j in range(c.NCH)])
    x_own = C(x[tok])
    nw = np.asarray(L0_attn_norm_w)[sl]
    nb = np.asarray(L0_attn_norm_b)[sl]
    xpwT = C(np.asarray(L0_attn_x_proj_w).T[sl])
    dsl = np.arange(g * c.DSL, (g + 1) * c.DSL)
    dtwT = C(np.asarray(L0_attn_dt_proj_w)[dsl].T)
    dtb = _pp(np.asarray(L0_attn_dt_proj_b)[dsl])
    alog = C(np.asarray(L0_attn_A_log)[dsl].reshape(c.NDG, 128, c.N)
             .transpose(1, 0, 2).reshape(128, c.NDG * c.N))
    dpar = _pp(np.asarray(L0_attn_D_param)[dsl])
    woutT = C(np.asarray(L0_attn_out_proj_w).T[dsl])
    wr9 = np.concatenate([np.asarray(L0_ffn_W_router),
                          np.asarray(L0_ffn_W_sgate)], axis=0)
    wr9T = C(wr9.T)
    esel = np.zeros((128, c.E), f32)
    esel[:, core] = 1.0
    eye = np.eye(128, dtype=f32)
    s0 = core * c.ISS
    wsg = np.asarray(L0_ffn_Ws_gate_up)
    ws_gu = C(np.concatenate([wsg[:, s0:s0 + c.ISS],
                              wsg[:, c.ISH + s0:c.ISH + s0 + c.ISS]], axis=1))
    ws_dn = C(np.asarray(L0_ffn_Ws_down)[s0:s0 + c.ISS])
    we_gu = C(np.asarray(L0_ffn_We_gate_up)[core])
    we_dn = C(np.asarray(L0_ffn_We_down)[core])
    return {
        "x_fm": x_fm, "x_own": x_own,
        "ln1w": C(_pp(nw)), "ln1b": C(_pp(nb)),
        "xpwT": xpwT, "dtwT": dtwT, "dtb": C(dtb), "alog": alog,
        "dpar": C(dpar), "woutT": woutT, "wr9T": wr9T, "esel": esel,
        "eye": eye, "ws_gu": ws_gu, "ws_dn": ws_dn, "we_gu": we_gu,
        "we_dn": we_dn,
    }


def assemble(cfg, results):
    c = cfg
    out = np.empty((c.B, c.T, c.D), np.float32)
    for core in range(NCORES):
        b, g = divmod(core, GRP)
        r = results[core]["out"]
        for j in range(c.NCH):
            out[b, j * c.TC + g * c.TPC:j * c.TC + (g + 1) * c.TPC] = \
                r[j * c.TPC:(j + 1) * c.TPC]
    return out


_CFG = Cfg()


def kernel(**inputs):
    c = _CFG
    bld = Builder(c, cc_inc=1)
    nc = bld.build()
    in_maps = [make_inputs(c, core, **inputs) for core in range(NCORES)]
    res = run_bass_kernel_spmd(nc, in_maps, core_ids=list(range(NCORES)))
    return assemble(c, res.results)
